# version 17
# baseline (speedup 1.0000x reference)
import numpy as np
from contextlib import ExitStack

import concourse.bass as bass
import concourse.mybir as mybir
from concourse.bass_utils import run_bass_kernel_spmd

F32 = mybir.dt.float32
BF16 = mybir.dt.bfloat16
Alu = mybir.AluOpType
Act = mybir.ActivationFunctionType
AxX = mybir.AxisListType.X

D = 1024
DI = 2048
DXZ = 4096
HF = 4096
DS = 16
DTR = 64
NCORES = 8
EPS = 1e-5
MAGIC = float(1.5 * 2.0 ** 23)
QB = 127.0

NDT = D // 128
NCH = DXZ // 128
NCHI = DI // 128
NH = HF // 128

ENGS = ("sync", "tensor", "vector", "scalar", "gpsimd")


class Prog:

    DMA_KEYS = ("sync", "gpsimd_dma")

    def __init__(self, nc, blk, es):
        self.nc = nc
        self.blk = blk
        keys = list(ENGS) + ["gpsimd_dma"]
        self.keys = keys
        self.queues = {e: [] for e in ENGS}
        self.cnt = {e: 0 for e in keys}
        self.sems = {e: es.enter_context(nc.semaphore("sem_" + e)) for e in keys}
        self.hwm = {e: {f: 0 for f in keys} for e in keys}
        self.fence_val = {e: 0 for e in keys}

    @staticmethod
    def _issuer(key):
        return "gpsimd" if key == "gpsimd_dma" else key

    def _fence(self, e):
        if self.fence_val[e] < self.cnt[e]:
            v = self.cnt[e]
            sem = self.sems[e]
            self.queues[self._issuer(e)].append(
                lambda engine, sem=sem, v=v: engine.wait_ge(sem, v))
            self.fence_val[e] = v
            ik = self._issuer(e)
            self.hwm[ik][e] = max(self.hwm[ik][e], v)
        return self.fence_val[e]

    def op(self, eng, fn, deps=(), dma=False):
        key = eng
        if dma and eng == "gpsimd":
            key = "gpsimd_dma"
        amt = 16 if (dma or eng == "sync") else 1
        need = {}
        for d in deps:
            if d is None:
                continue
            e, c = d
            if e in self.DMA_KEYS:
                if self._issuer(e) == eng and c >= self.fence_val[e]:
                    c = self.cnt[e]
                    self.fence_val[e] = c
                elif c > self.fence_val[e]:
                    c = self._fence(e)
                else:
                    c = self.fence_val[e] if c > 0 else 0
            need[e] = max(need.get(e, 0), c)
        waits = []
        for e, c in need.items():
            if c > self.hwm[key][e]:
                waits.append((self.sems[e], c))
                self.hwm[key][e] = c
        self.cnt[key] += amt
        sem = self.sems[key]

        def emit(engine, waits=waits, fn=fn, sem=sem, amt=amt):
            for s, c in waits:
                engine.wait_ge(s, c)
            fn(engine).then_inc(sem, amt)

        self.queues[eng].append(emit)
        return (key, self.cnt[key])

    def raw(self, eng, fn):
        self.queues[eng].append(fn)

    def flush(self):
        for e in ENGS:
            q = self.queues[e]
            if q:
                getattr(self.blk, e)(lambda engine, q=q: [f(engine) for f in q])
            self.queues[e] = []


def col_layout(x):
    CH, k = x.shape
    nt = CH // 128
    return np.ascontiguousarray(x.reshape(nt, 128, k).transpose(1, 0, 2).reshape(128, nt * k))


def build_nc(tc, stage, s1cut=99):
    ntok = tc // 128
    tp = ((tc + 3 + 127) // 128) * 128
    ntokp = tp // 128
    ncw = (tp + 511) // 512
    nct = tc // 512
    ttot = float(tc * NCORES)

    nc = bass.Bass("TRN2", target_bir_lowering=False)

    dget = lambda n, s: nc.dram_tensor(n, s, F32, kind="ExternalInput").ap()
    dgetb = lambda n, s: nc.dram_tensor(n, s, BF16, kind="ExternalInput").ap()
    doutf = lambda n, s: nc.dram_tensor(n, s, F32, kind="ExternalOutput").ap()
    doutb = lambda n, s: nc.dram_tensor(n, s, BF16, kind="ExternalOutput").ap()

    ident_in = dget("ident", [128, 128])
    if stage == 1:
        x_tok = dget("x_tok", [tp, D])
        w_in_t = dget("w_in_t", [D, DXZ])
        ln_w_col = dget("ln_w_col", [128, NDT])
        ln_b_col = dget("ln_b_col", [128, NDT])
        in_b_col = dget("in_b_col", [128, NCH])
        convw_col = dget("convw_col", [128, NCH * 4])
        convb_col = dget("convb_col", [128, NCH])
        aproj_t = dget("aproj_t", [DI, DTR])
        dtw_t = dget("dtw_t", [DTR, DI])
        dtb_col = dget("dtb_col", [128, NCHI])
        w1_sh = dget("w1_sh", [128, HF])
        w2_sh = dget("w2_sh", [HF // NCORES, D])
        halo_mask = dget("halo_mask", [128, 4])
        ones_row_in = dget("ones_row", [1, 128])
        xs_io = doutb("xs_io", [NCHI, 128, tc])
        z_io = doutb("z_io", [NCHI, 128, tc])
        dtp_io = doutf("dtp_io", [128, NCHI])
        wpart_io = doutf("wpart_io", [128, 8])
    if stage in (2, 3):
        alog_col = dget("alog_col", [128, NCHI * DS])
        dt_all = dget("dt_all", [NCORES, 128, NCHI])
        xs_in = dgetb("xs_io", [NCHI, 128, tc])
    if stage == 2:
        f_out = doutf("f_io", [128, NCHI * DS])
        wpart_all = dget("wpart_all", [NCORES, 128, 8])
        revramp = dget("revramp", [128, tc])
        w1_sh = dget("w1_sh", [128, HF])
        w2_sh = dget("w2_sh", [HF // NCORES, D])
        ones_row_in = dget("ones_row", [1, 128])
        w1q_io = doutb("w1q_io", [1, 128, HF])
        w2q_io = doutb("w2q_io", [NH // NCORES, 128, D])
        wsc_io = doutf("wsc_io", [128, 8])
    if stage == 3:
        bmat_col = dget("bmat_col", [128, NCHI * DS])
        cmat_col = dget("cmat_col", [128, NCHI * DS])
        dvec_col = dget("dvec_col", [128, NCHI])
        chain_e = dget("chain_e", [128, NCORES])
        f_all = dget("f_all", [NCORES, 128, NCHI * DS])
        z_in = dgetb("z_io", [NCHI, 128, tc])
        x_res = dget("x_res", [tc, D])
        outw_t = dget("outw_t", [DI, D])
        outb_rep = dget("outb_rep", [128, D])
        b1_rep = dget("b1_rep", [128, HF])
        b2_rep = dget("b2_rep", [128, D])
        w1q_in = dgetb("w1q_io", [NDT, 128, HF])
        w2q_in = dgetb("w2q_io", [NH, 128, D])
        wsc_in = dget("wsc_io", [128, 8])
        out_d = doutf("out", [tc, D])
        x1_spill = nc.dram_tensor("x1_spill", [ntok, 128, D], F32, kind="Internal").ap()

    es = ExitStack()
    sb = lambda n, s, dt: es.enter_context(nc.sbuf_tensor(n, s, dt))
    psf = lambda n, s, dt=F32: es.enter_context(nc.psum_tensor(n, s, dt))

    blk = es.enter_context(nc.Block())
    P = Prog(nc, blk, es)

    ident_f = sb("ident_f", [128, 128], F32)
    ident_b = sb("ident_b", [128, 128], BF16)
    stat6 = sb("stat6", [128, 12], F32)
    mv_s = sb("mv_s", [128, 2], F32)
    mv2_s = sb("mv2_s", [128, 2], F32)
    strips = sb("strips", [128, 8], F32)
    dtav_s = sb("dtav_s", [128, NCHI], F32)
    if stage == 3:
        wsc_s = sb("wsc_s", [128, 8], F32)
        outb_s = sb("outb_s", [128, D], F32)
        b2_s = sb("b2_s", [128, D], F32)
        stripsR = [sb(f"stripsR{i}", [128, 8], F32) for i in range(2)]
        stat6R = [sb(f"stat6R{i}", [128, 12], F32) for i in range(2)]
        mvR = [sb(f"mvR{i}", [128, 2], F32) for i in range(2)]

    pA = psf("pA", [128, 512])
    pB = psf("pB", [128, 512])
    pT = [psf(f"pT{i}", [128, 128], BF16) for i in range(2)]
    if stage == 1:
        pS = psf("pS", [128, 16])
    if stage == 3:
        pYb = [psf(f"pY{i}", [128, 512]) for i in range(4)]

    t_identl = P.op("sync", lambda s: s.dma_start(out=ident_f[:], in_=ident_in[:, :]), dma=True)
    t_identb = P.op("vector", lambda v: v.tensor_copy(out=ident_b[:], in_=ident_f[:]),
                    deps=[t_identl])

    if stage == 1:
        onesrow_s = sb("onesrow_s", [1, 128], F32)
        onescol_s = sb("onescol_s", [128, 1], F32)
        lnw_s = sb("lnw_s", [128, NDT], F32)
        lnb_s = sb("lnb_s", [128, NDT], F32)
        lnb_b = sb("lnb_b", [128, NDT], BF16)
        inb_s = sb("inb_s", [128, NCH], F32)
        cvec_s = sb("cvec_s", [128, NCH], F32)
        convw_s = sb("convw_s", [128, NCH * 4], F32)
        convb_s = sb("convb_s", [128, NCH], F32)
        dtb_s = sb("dtb_s", [128, NCHI], F32)
        halo_s = sb("halo_s", [128, 4], F32)
        dts_strip = sb("dts_strip", [128, NCHI * 8], F32)
        wsc_s = sb("wsc_s", [128, 8], F32)
        wstrip = sb("wstrip", [128, 128], F32)
        xs_sb = sb("xs_sb", [128, NCHI, tc], BF16)
        diagb = [sb(f"diag{i}", [128, 128], BF16) for i in range(2)]
        esW = ExitStack()
        sbW = lambda n, s, dt: esW.enter_context(nc.sbuf_tensor(n, s, dt))
        w_half = sbW("w_half", [128, NDT, DI], BF16)
        v_T = sbW("v_T", [128, NDT, tp], BF16)
        wstg = [sbW(f"wstg{i}", [128, 1024], F32) for i in range(2)]
        xtile = [sbW("xtl0", [128, D], F32)] * 2
        u_f = sbW("u_f", [128, D], F32)
        ub_b = sbW("ub_b", [128, D], BF16)
        xz_t = sbW("xz_t", [128, tp + 8], BF16)
        zrot = sbW("zrot", [128, tc], BF16)
        ai_b = sbW("ai_b", [128, tc], BF16)
        aprj_b = sbW("aprj_b", [128, NCHI, DTR], BF16)
        dtw_b = sbW("dtw_b", [128, DI], BF16)
        junkA = sbW("junkA", [128, 1024], F32)

        t_small = {}
        for key, dst, srcd in (
            ("onesrow", onesrow_s, ones_row_in), ("lnw", lnw_s, ln_w_col),
            ("lnb", lnb_s, ln_b_col), ("inb", inb_s, in_b_col),
            ("convw", convw_s, convw_col), ("convb", convb_s, convb_col),
            ("dtb", dtb_s, dtb_col), ("halo", halo_s, halo_mask),
        ):
            t_small[key] = P.op("sync", lambda s, dst=dst, srcd=srcd:
                                s.dma_start(out=dst[:], in_=srcd[:, :]), dma=True)
        t_ones = P.op("vector", lambda v: v.memset(onescol_s[:], 1.0))
        t_wsc0 = P.op("vector", lambda v: v.memset(wsc_s[:], 0.0))

        if s1cut < 1:
            fin_s = P.cnt["sync"]
            P.raw("sync", lambda engine, v=fin_s: engine.wait_ge(P.sems["sync"], v))
            fin_v = P.cnt["vector"]
            P.raw("vector", lambda engine, v=fin_v: engine.wait_ge(P.sems["vector"], v))
            P.flush()
            es.close()
            return nc
        t_lnbb = P.op("vector", lambda v: v.tensor_copy(out=lnb_b[:], in_=lnb_s[:]),
                      deps=[t_small["lnb"]])

        tr_tok = []
        trp_last = None
        uo_prev = None
        for tt in range(ntokp):
            i = tt % 2
            xt = xtile[i]
            tld = P.op("sync", lambda s, tt=tt, i=i: s.dma_start(
                out=xtile[i][:], in_=x_tok[tt * 128:(tt + 1) * 128, :]),
                deps=[uo_prev], dma=True)
            b1 = P.op("vector", lambda v, xt=xt: v.bn_stats(out=stat6[:, 0:6], in_=xt[:, 0:512]),
                      deps=[tld])
            b2t = P.op("vector", lambda v, xt=xt: v.bn_stats(out=stat6[:, 6:12], in_=xt[:, 512:1024]),
                       deps=[tld])
            ag = P.op("vector", lambda v: v.bn_aggr(out=mv_s[:], in_=stat6[:, 0:12]),
                      deps=[b1, b2t])
            ve = P.op("vector", lambda v: v.tensor_scalar(
                out=strips[:, 6:7], in0=mv_s[:, 1:2], scalar1=EPS, scalar2=None, op0=Alu.add),
                deps=[ag])
            rs0 = P.op("scalar", lambda s: s.activation(out=strips[:, 7:8], in_=strips[:, 6:7],
                                                        func=Act.Sqrt), deps=[ve])
            rs = P.op("vector", lambda v: v.reciprocal(out=strips[:, 7:8], in_=strips[:, 7:8]),
                      deps=[rs0])
            uo = P.op("vector", lambda v, xt=xt: v.tensor_scalar(
                out=u_f[:], in0=xt[:], scalar1=mv_s[:, 0:1], scalar2=strips[:, 7:8],
                op0=Alu.subtract, op1=Alu.mult), deps=[rs])
            uo_prev = uo
            b3 = P.op("vector", lambda v: v.bn_stats(out=stat6[:, 0:6], in_=u_f[:, 0:512]),
                      deps=[uo])
            b4 = P.op("vector", lambda v: v.bn_stats(out=stat6[:, 6:12], in_=u_f[:, 512:1024]),
                      deps=[uo])
            ag2 = P.op("vector", lambda v: v.bn_aggr(out=mv2_s[:], in_=stat6[:, 0:12]),
                       deps=[b3, b4])
            ve2 = P.op("vector", lambda v: v.tensor_scalar(
                out=strips[:, 6:7], in0=mv2_s[:, 1:2], scalar1=EPS, scalar2=None, op0=Alu.add),
                deps=[ag2])
            rs20 = P.op("scalar", lambda s: s.activation(out=strips[:, 7:8], in_=strips[:, 6:7],
                                                         func=Act.Sqrt), deps=[ve2])
            rs2 = P.op("vector", lambda v: v.reciprocal(out=strips[:, 7:8], in_=strips[:, 7:8]),
                       deps=[rs20])
            ub = P.op("vector", lambda v: v.tensor_scalar(
                out=ub_b[:], in0=u_f[:], scalar1=mv2_s[:, 0:1], scalar2=strips[:, 7:8],
                op0=Alu.subtract, op1=Alu.mult),
                deps=[rs2] + ([trp_last] if tr_tok else []))
            for kd in range(NDT):
                trp = P.op("tensor", lambda t, kd=kd: t.transpose(
                    out=pT[kd % 2][:, :],
                    in_=ub_b[:, kd * 128:(kd + 1) * 128], identity=ident_b[:]),
                    deps=[ub, t_identb] + ([tr_tok[-2]] if len(tr_tok) >= 2 else []))
                trp_last = trp
                cv = P.op("vector", lambda v, tt=tt, kd=kd: v.tensor_copy(
                    out=v_T[:, kd, tt * 128:(tt + 1) * 128],
                    in_=pT[kd % 2][:, :]), deps=[trp])
                tr_tok.append(cv)
        t_vT = tr_tok[-1]

        if s1cut < 2:
            fin_s = P.cnt["sync"]
            P.raw("sync", lambda engine, v=fin_s: engine.wait_ge(P.sems["sync"], v))
            fin_v = P.cnt["vector"]
            P.raw("vector", lambda engine, v=fin_v: engine.wait_ge(P.sems["vector"], v))
            P.flush()
            es.close()
            return nc

        def attn_half(half, dep_start):
            ch0 = half * NCHI
            tcast = []
            prevw = [dep_start, dep_start]
            for kd in range(NDT):
                tcst = None
                for hh in range(2):
                    i = (2 * kd + hh) % 2
                    tl = P.op("sync", lambda s, kd=kd, i=i, hh=hh, half=half: s.dma_start(
                        out=wstg[i][:], in_=w_in_t[kd * 128:(kd + 1) * 128,
                                                   half * DI + hh * 1024:half * DI + (hh + 1) * 1024]),
                        deps=[prevw[i]], dma=True)
                    tc2_ = P.op("vector", lambda v, kd=kd, i=i, hh=hh: v.tensor_copy(
                        out=w_half[:, kd, hh * 1024:(hh + 1) * 1024], in_=wstg[i][:]), deps=[tl])
                    prevw[i] = tc2_
                    tcst = tc2_ if tcst is None else (tc2_ if tc2_[1] > tcst[1] else tcst)
                tcast.append(tcst)
            tcv = []
            for m in range(NCHI):
                mm = None
                for kd in range(NDT):
                    mm = P.op("tensor", lambda t, m=m, kd=kd: t.matmul(
                        out=pS[:, 8:9], lhsT=w_half[:, kd, m * 128:(m + 1) * 128],
                        rhs=lnb_b[:, kd:kd + 1], start=(kd == 0), stop=(kd == NDT - 1)),
                        deps=[tcast[kd], t_lnbb] + (tcv[-1:] if tcv else []))
                tcv.append(P.op("vector", lambda v, m=m, ch0=ch0: v.tensor_tensor(
                    out=cvec_s[:, ch0 + m:ch0 + m + 1], in0=pS[:, 8:9],
                    in1=inb_s[:, ch0 + m:ch0 + m + 1], op=Alu.add),
                    deps=[mm, t_small["inb"]]))
            tsc = []
            for kd in range(NDT):
                tsc.append(P.op("vector", lambda v, kd=kd: v.tensor_scalar(
                    out=w_half[:, kd, :], in0=w_half[:, kd, :], scalar1=lnw_s[:, kd:kd + 1],
                    scalar2=None, op0=Alu.mult),
                    deps=[tcast[kd], tcv[-1], t_small["lnw"]]))

            out_tok = []
            prev_conv_last = None
            for m in range(NCHI):
                chg = ch0 + m
                evs = []
                for ckn in range(ncw):
                    c0 = ckn * 512
                    cw = min(512, tp - c0)
                    pp = pA if ckn % 2 == 0 else pB
                    mm = None
                    for kd in range(NDT):
                        ipdeps = [tsc[kd], t_vT]
                        if len(evs) >= 2:
                            ipdeps.append(evs[-2])
                        elif out_tok:
                            ipdeps.append(out_tok[-1])
                        mm = P.op("tensor", lambda t, m=m, kd=kd, c0=c0, cw=cw, pp=pp: t.matmul(
                            out=pp[:, 0:cw], lhsT=w_half[:, kd, m * 128:(m + 1) * 128],
                            rhs=v_T[:, kd, c0:c0 + cw], start=(kd == 0), stop=(kd == NDT - 1)),
                            deps=ipdeps)
                    evs.append(P.op("vector", lambda v, c0=c0, cw=cw, pp=pp, chg=chg: v.tensor_scalar(
                        out=xz_t[:, c0:c0 + cw], in0=pp[:, 0:cw],
                        scalar1=cvec_s[:, chg:chg + 1], scalar2=None, op0=Alu.add),
                        deps=[mm] + ([prev_conv_last] if ckn == 0 and prev_conv_last else [])))
                hm = P.op("vector", lambda v: v.tensor_tensor(
                    out=xz_t[:, 0:3], in0=xz_t[:, 0:3], in1=halo_s[:, 0:3], op=Alu.mult),
                    deps=[evs[0], t_small["halo"]])
                last_silu = None
                conv_mms = []
                for ckn in range(nct):
                    c0 = ckn * 512
                    pp = pA if ckn % 2 == 0 else pB
                    mm = None
                    for j in range(4):
                        idx = ckn * 4 + j
                        djdeps = [t_identb, t_small["convw"], evs[-1], hm]
                        if len(conv_mms) >= 2:
                            djdeps.append(conv_mms[-2])
                        dj = P.op("vector", lambda v, j=j, chg=chg, idx=idx: v.tensor_scalar(
                            out=diagb[idx % 2][:], in0=ident_b[:],
                            scalar1=convw_s[:, chg * 4 + j:chg * 4 + j + 1],
                            scalar2=None, op0=Alu.mult), deps=djdeps)
                        mmdeps = [dj, evs[-1], hm]
                        if last_silu is not None and j == 0:
                            mmdeps.append(last_silu)
                        mm = P.op("tensor", lambda t, j=j, c0=c0, pp=pp, idx=idx: t.matmul(
                            out=pp[:, 0:512], lhsT=diagb[idx % 2][:],
                            rhs=xz_t[:, c0 + j:c0 + j + 512], start=(j == 0), stop=(j == 3)),
                            deps=mmdeps)
                        conv_mms.append(mm)
                    prev_conv_last = mm
                    if half == 0:
                        last_silu = P.op("scalar", lambda s, m=m, c0=c0, pp=pp, chg=chg:
                                         s.activation(out=xs_sb[:, m, c0:c0 + 512], in_=pp[:, 0:512],
                                                      func=Act.Silu, bias=convb_s[:, chg:chg + 1],
                                                      scale=1.0), deps=[mm, t_small["convb"]])
                    else:
                        last_silu = P.op("scalar", lambda s, c0=c0, pp=pp, chg=chg:
                                         s.activation(out=zrot[:, c0:c0 + 512], in_=pp[:, 0:512],
                                                      func=Act.Silu, bias=convb_s[:, chg:chg + 1],
                                                      scale=1.0), deps=[mm, t_small["convb"]])
                if half == 1:
                    sp = P.op("sync", lambda s, m=m: s.dma_start(
                        out=z_io[m, :, :], in_=zrot[:]), deps=[last_silu], dma=True)
                    out_tok.append(sp)
                else:
                    out_tok.append(last_silu)
            return out_tok

        t_xs = attn_half(0, t_vT)

        if s1cut < 3:
            fin_s = P.cnt["sync"]
            P.raw("sync", lambda engine, v=fin_s: engine.wait_ge(P.sems["sync"], v))
            fin_a = P.cnt["scalar"]
            P.raw("scalar", lambda engine, v=fin_a: engine.wait_ge(P.sems["scalar"], v))
            P.flush()
            es.close()
            return nc

        t_xsout = []
        for m in range(NCHI):
            t_xsout.append(P.op("sync", lambda s, m=m: s.dma_start(
                out=xs_io[m, :, :], in_=xs_sb[:, m, :]), deps=[t_xs[m]], dma=True))

        tap = []
        prev = None
        for m in range(NCHI):
            tl = P.op("sync", lambda s, m=m: s.dma_start(
                out=wstg[0][:, 0:DTR], in_=aproj_t[m * 128:(m + 1) * 128, :]),
                deps=[prev] if prev else [t_xs[-1]], dma=True)
            prev = P.op("vector", lambda v, m=m: v.tensor_copy(
                out=aprj_b[:, m, :], in_=wstg[0][:, 0:DTR]), deps=[tl])
            tap.append(prev)
        for hh in range(2):
            tdw_l = P.op("sync", lambda s, hh=hh: s.dma_start(
                out=wstg[1][0:DTR, :], in_=dtw_t[:, hh * 1024:(hh + 1) * 1024]),
                deps=[prev], dma=True)
            prev = P.op("vector", lambda v, hh=hh: v.tensor_copy(
                out=dtw_b[0:DTR, hh * 1024:(hh + 1) * 1024], in_=wstg[1][0:DTR, :]),
                deps=[tdw_l])
        tdw = prev

        tai = []
        for ckn in range(nct):
            c0 = ckn * 512
            mm = None
            for m in range(NCHI):
                mm = P.op("tensor", lambda t, m=m, c0=c0: t.matmul(
                    out=pA[0:DTR, 0:512], lhsT=aprj_b[:, m, :], rhs=xs_sb[:, m, c0:c0 + 512],
                    start=(m == 0), stop=(m == NCHI - 1)),
                    deps=[tap[m], t_xs[m]] + (tai[-1:] if tai else []))
            tai.append(P.op("vector", lambda v, c0=c0: v.tensor_copy(
                out=ai_b[0:DTR, c0:c0 + 512], in_=pA[0:DTR, 0:512]), deps=[mm]))

        tdt = []
        for m in range(NCHI):
            for pair in range(max(1, nct // 2)):
                texp = []
                for sub in range(min(2, nct)):
                    ckn = pair * 2 + sub
                    c0 = ckn * 512
                    pp = pA if ckn % 2 == 0 else pB
                    mm = P.op("tensor", lambda t, m=m, c0=c0, pp=pp: t.matmul(
                        out=pp[:, 0:512], lhsT=dtw_b[0:DTR, m * 128:(m + 1) * 128],
                        rhs=ai_b[0:DTR, c0:c0 + 512], start=True, stop=True),
                        deps=[tdw, tai[ckn], tai[-1]] + (tdt[-1:] if tdt else []))
                    texp.append(P.op("scalar", lambda s, m=m, sub=sub, pp=pp: s.activation(
                        out=junkA[:, sub * 512:(sub + 1) * 512], in_=pp[:, 0:512], func=Act.Exp,
                        bias=dtb_s[:, m:m + 1], scale=1.0), deps=[mm, t_small["dtb"]]))
                for sub in range(min(2, nct)):
                    ckn = pair * 2 + sub
                    tdt.append(P.op("scalar", lambda s, m=m, ckn=ckn, sub=sub: s.activation(
                        out=u_f[:, 0:512], in_=junkA[:, sub * 512:(sub + 1) * 512], func=Act.Ln,
                        bias=1.0, scale=1.0,
                        accum_out=dts_strip[:, m * 8 + ckn:m * 8 + ckn + 1]),
                        deps=[texp[sub]]))
        t_dtred = P.op("vector", lambda v: v.tensor_reduce(
            out=dtav_s.ap().rearrange("p (m o) -> p m o", o=1),
            in_=dts_strip.ap().rearrange("p (m c) -> p m c", c=8)[:, :, 0:nct],
            axis=AxX, op=Alu.add), deps=tdt)
        P.op("sync", lambda s: s.dma_start(out=dtp_io[:, :], in_=dtav_s[:]),
             deps=[t_dtred], dma=True)

        if s1cut < 4:
            fin_s = P.cnt["sync"]
            P.raw("sync", lambda engine, v=fin_s: engine.wait_ge(P.sems["sync"], v))
            P.flush()
            es.close()
            return nc

        t_z = attn_half(1, t_dtred)

        if s1cut < 5:
            fin_s = P.cnt["sync"]
            P.raw("sync", lambda engine, v=fin_s: engine.wait_ge(P.sems["sync"], v))
            P.flush()
            es.close()
            return nc

        esW.close()
        barW = [(e, P.cnt[e]) for e in P.keys if P.cnt[e] > 0]
        NWB = 4
        wqst = [sb(f"wqst{i}", [128, 1024], F32) for i in range(NWB)]

        wq_abs_tok = []
        prevq = [None] * NWB
        first_wq = True
        kq = 0
        for wid, (w_dram, n_tiles) in enumerate(((w1_sh, 1), (w2_sh, 4))):
            colw = w_dram.shape[1]
            nck = max(1, colw // 1024)
            ckw = min(colw, 1024)
            for t in range(n_tiles):
                for ck in range(nck):
                    i = kq % NWB
                    tl = P.op("gpsimd", lambda g, st=wqst[i], w=w_dram, t=t, ck=ck, ckw=ckw:
                              g.dma_start(out=st[:, 0:ckw],
                                          in_=w[t * 128:(t + 1) * 128, ck * ckw:(ck + 1) * ckw]),
                              deps=[prevq[i]] + (barW if first_wq else []), dma=True)
                    first_wq = False
                    col = wid * 4 + t * nck + ck
                    ta = P.op("vector", lambda v, st=wqst[i], ckw=ckw, col=col:
                              v.tensor_reduce(out=wstrip[:, col:col + 1], in_=st[:, 0:ckw],
                                              axis=AxX, op=Alu.add,
                                              apply_absolute_value=True),
                              deps=[tl])
                    prevq[i] = ta
                    wq_abs_tok.append(ta)
                    kq += 1
        P.op("sync", lambda s: s.dma_start(out=wpart_io[:, :], in_=wstrip[:, 0:8]),
             deps=wq_abs_tok, dma=True)

        fin_s = P.cnt["sync"]
        P.raw("sync", lambda engine, v=fin_s: engine.wait_ge(P.sems["sync"], v))
        fin_g = P.cnt["gpsimd_dma"]
        P.raw("gpsimd", lambda engine, v=fin_g: engine.wait_ge(P.sems["gpsimd_dma"], v))

    if stage in (2, 3):
        esA3 = ExitStack()
        sb3 = lambda n, s, dt: esA3.enter_context(nc.sbuf_tensor(n, s, dt))
        alog_s = sb3("alog_s", [128, NCHI * DS], F32)
        g_col = sb3("g_col", [128, NCHI * DS], F32)
        a_col = sb3("a_col", [128, NCHI * DS], F32)
        tmpc1 = sb3("tmpc1", [128, NCHI * DS], F32)
        dtall_s = sb3("dtall_s", [128, NCORES, NCHI], F32)
        xs_sb2 = sb3("xs_sb2", [128, NCHI, tc], BF16)

        t_alog = P.op("sync", lambda s: s.dma_start(out=alog_s[:], in_=alog_col[:, :]), dma=True)
        t_dta = P.op("sync", lambda s: s.dma_start(
            out=dtall_s[:, :, :],
            in_=dt_all.rearrange("j p m -> p j m")), dma=True)
        t_xsl = []
        for m in range(NCHI):
            t_xsl.append(P.op("sync", lambda s, m=m: s.dma_start(
                out=xs_sb2[:, m, :], in_=xs_in[m, :, :]), dma=True))

        t_dts = P.op("vector", lambda v: v.tensor_copy(
            out=dtav_s[:], in_=dtall_s[:, 0, :]), deps=[t_dta])
        for j in range(1, NCORES):
            t_dts = P.op("vector", lambda v, j=j: v.tensor_tensor(
                out=dtav_s[:], in0=dtav_s[:], in1=dtall_s[:, j, :], op=Alu.add),
                deps=[t_dts, t_dta])
        t_dtav = P.op("vector", lambda v: v.tensor_scalar(
            out=dtav_s[:], in0=dtav_s[:], scalar1=1.0 / ttot, scalar2=None, op0=Alu.mult),
            deps=[t_dts])

        te_ = P.op("scalar", lambda s: s.activation(out=tmpc1[:], in_=alog_s[:], func=Act.Exp),
                   deps=[t_alog])
        tg = []
        for m in range(NCHI):
            tg.append(P.op("vector", lambda v, m=m: v.tensor_scalar(
                out=g_col[:, m * DS:(m + 1) * DS], in0=tmpc1[:, m * DS:(m + 1) * DS],
                scalar1=dtav_s[:, m:m + 1], scalar2=-1.0, op0=Alu.mult, op1=Alu.mult),
                deps=[te_, t_dtav]))
        t_a = P.op("scalar", lambda s: s.activation(out=a_col[:], in_=g_col[:], func=Act.Exp),
                   deps=tg)

    if stage == 2:
        apb = [sb3(f"apb{i}", [128, tc], F32) for i in range(2)]
        mtb = [sb3(f"mtb{i}", [128, tc], F32) for i in range(2)]
        dump2 = sb3("dump2", [128, tc], F32)
        ramp_s = sb3("ramp_s", [128, tc], F32)
        floc_s = sb3("floc_s", [128, NCHI * DS], F32)
        t_rmp = P.op("sync", lambda s: s.dma_start(
            out=ramp_s[:], in_=revramp[:, :]), dma=True)
        prevM = [None, None]
        prevAcc = [None, None]
        last_dve = None
        last_act = None
        for m in range(NCHI):
            dve_path = (m % 3 == 0)
            for s_ in range(DS):
                col = m * DS + s_
                ib = col % 2
                e_ = P.op("scalar", lambda s, ib=ib, col=col: s.activation(
                    out=apb[ib][:], in_=ramp_s[:], func=Act.Exp,
                    scale=g_col[:, col:col + 1]),
                    deps=[t_rmp, tg[m], prevM[ib]])
                mu = P.op("vector", lambda v, ib=ib, m=m: v.tensor_tensor(
                    out=mtb[ib][:], in0=apb[ib][:], in1=xs_sb2[:, m, :], op=Alu.mult),
                    deps=[e_, t_xsl[m], prevAcc[ib]])
                prevM[ib] = mu
                if dve_path:
                    acc = P.op("vector", lambda v, ib=ib, col=col: v.tensor_reduce(
                        out=floc_s[:, col:col + 1], in_=mtb[ib][:], axis=AxX,
                        op=Alu.add), deps=[mu, last_dve])
                    last_dve = acc
                else:
                    acc = P.op("scalar", lambda s, ib=ib, col=col: s.activation(
                        out=dump2[:], in_=mtb[ib][:], func=Act.Identity,
                        accum_out=floc_s[:, col:col + 1]), deps=[mu, last_act])
                    last_act = acc
                prevAcc[ib] = acc
        P.op("sync", lambda s: s.dma_start(out=f_out[:, :], in_=floc_s[:]),
             deps=[last_dve, last_act], dma=True)

        wp_s = sb3("wp_s", [128, NCORES, 8], F32)
        wstr2 = sb3("wstr2", [128, 8], F32)
        str8 = sb3("str8", [128, 8], F32)
        wsc2 = sb3("wsc2", [128, 8], F32)
        ones_c = sb3("ones_c", [128, 1], F32)
        ones_r = sb3("ones_r", [1, 128], F32)
        wq2st = [sb3(f"wq2st{i}", [128, 1024], F32) for i in range(2)]
        wq2tmp = sb3("wq2tmp", [128, 1024], F32)
        wq2sg = sb3("wq2sg", [128, 1024], F32)
        wq2out = [sb3(f"wq2out{i}", [128, 1024], BF16) for i in range(2)]
        pS2 = psf("pS2", [128, 8])

        t_wp = P.op("sync", lambda s: s.dma_start(
            out=wp_s[:, :, :], in_=wpart_all.rearrange("j p m -> p j m")), dma=True)
        t_or = P.op("sync", lambda s: s.dma_start(out=ones_r[:, :], in_=ones_row_in[:, :]), dma=True)
        t_oc = P.op("vector", lambda v: v.memset(ones_c[:], 1.0))
        t_w0 = P.op("vector", lambda v: v.memset(wsc2[:], 0.0))
        tsum = P.op("vector", lambda v: v.tensor_copy(out=wstr2[:], in_=wp_s[:, 0, :]),
                    deps=[t_wp])
        for j in range(1, NCORES):
            tsum = P.op("vector", lambda v, j=j: v.tensor_tensor(
                out=wstr2[:], in0=wstr2[:], in1=wp_s[:, j, :], op=Alu.add),
                deps=[tsum, t_wp])
        wq_scale_tok = [None, None]
        for wid in range(2):
            nel = float(D * HF)
            tr = P.op("vector", lambda v, wid=wid: v.tensor_reduce(
                out=str8[:, wid:wid + 1], in_=wstr2[:, wid * 4: wid * 4 + 4],
                axis=AxX, op=Alu.add), deps=[tsum])
            tm = P.op("tensor", lambda t, wid=wid: t.matmul(
                out=pS2[0:1, wid:wid + 1], lhsT=ones_c[:], rhs=str8[:, wid:wid + 1],
                start=True, stop=True), deps=[tr, t_oc])
            ts1 = P.op("vector", lambda v, wid=wid, nel=nel: v.tensor_scalar(
                out=str8[0:1, 2 + wid:3 + wid], in0=pS2[0:1, wid:wid + 1],
                scalar1=1.0 / nel, scalar2=EPS, op0=Alu.mult, op1=Alu.max), deps=[tm])
            ts2 = P.op("vector", lambda v, wid=wid: v.reciprocal(
                out=str8[0:1, 4 + wid:5 + wid], in_=str8[0:1, 2 + wid:3 + wid]),
                deps=[ts1])
            tb1 = P.op("tensor", lambda t, wid=wid: t.matmul(
                out=pS2[:, 2 + 2 * wid:3 + 2 * wid], lhsT=ones_r[0:1, :],
                rhs=str8[0:1, 2 + wid:3 + wid], start=True, stop=True),
                deps=[ts1, t_or])
            tb2 = P.op("tensor", lambda t, wid=wid: t.matmul(
                out=pS2[:, 3 + 2 * wid:4 + 2 * wid], lhsT=ones_r[0:1, :],
                rhs=str8[0:1, 4 + wid:5 + wid], start=True, stop=True), deps=[ts2])
            tc1 = P.op("vector", lambda v, wid=wid: v.tensor_copy(
                out=wsc2[:, 2 * wid:2 * wid + 1], in_=pS2[:, 2 + 2 * wid:3 + 2 * wid]),
                deps=[tb1, t_w0])
            tc2 = P.op("vector", lambda v, wid=wid: v.tensor_copy(
                out=wsc2[:, 2 * wid + 1:2 * wid + 2], in_=pS2[:, 3 + 2 * wid:4 + 2 * wid]),
                deps=[tb2])
            wq_scale_tok[wid] = (tc1, tc2)
        P.op("sync", lambda s: s.dma_start(out=wsc_io[:, :], in_=wsc2[:]),
             deps=[wq_scale_tok[0][1], wq_scale_tok[1][1]], dma=True)

        prev_t4 = None
        prev_sp = [None, None]
        prevld = [None, None]
        kq = 0
        for wid, (w_dram, n_tiles, spill) in enumerate(((w1_sh, 1, w1q_io), (w2_sh, 4, w2q_io))):
            colw = w_dram.shape[1]
            nck = max(1, colw // 1024)
            ckw = min(colw, 1024)
            for t in range(n_tiles):
                for ck in range(nck):
                    i = kq % 2
                    tl = P.op("gpsimd", lambda g, st=wq2st[i], w=w_dram, t=t, ck=ck, ckw=ckw:
                              g.dma_start(out=st[:, 0:ckw],
                                          in_=w[t * 128:(t + 1) * 128, ck * ckw:(ck + 1) * ckw]),
                              deps=[prevld[i]], dma=True)
                    t1 = P.op("vector", lambda v, st=wq2st[i], ckw=ckw, wid=wid:
                              v.tensor_scalar(out=wq2tmp[:, 0:ckw], in0=st[:, 0:ckw],
                                              scalar1=wsc2[:, 2 * wid + 1:2 * wid + 2],
                                              scalar2=None, op0=Alu.mult),
                              deps=[tl, wq_scale_tok[wid][1], prev_t4])
                    t2 = P.op("vector", lambda v, ckw=ckw:
                              v.tensor_scalar(out=wq2sg[:, 0:ckw], in0=wq2tmp[:, 0:ckw],
                                              scalar1=0.5, scalar2=None, op0=Alu.is_ge),
                              deps=[t1])
                    t3 = P.op("vector", lambda v, ckw=ckw:
                              v.tensor_scalar(out=wq2tmp[:, 0:ckw], in0=wq2tmp[:, 0:ckw],
                                              scalar1=-0.5, scalar2=None, op0=Alu.is_le),
                              deps=[t2])
                    t4 = P.op("vector", lambda v, ckw=ckw, ib=kq % 2:
                              v.tensor_tensor(out=wq2out[ib][:, 0:ckw], in0=wq2sg[:, 0:ckw],
                                              in1=wq2tmp[:, 0:ckw], op=Alu.subtract),
                              deps=[t3, prev_sp[kq % 2]])
                    t5 = P.op("gpsimd", lambda g, sp=spill, t=t, ck=ck, ckw=ckw, ib=kq % 2:
                              g.dma_start(out=sp[t, :, ck * ckw:(ck + 1) * ckw],
                                          in_=wq2out[ib][:, 0:ckw]), deps=[t4], dma=True)
                    prev_sp[kq % 2] = t5
                    prev_t4 = t4
                    prevld[i] = t4
                    kq += 1

        esA3.close()
        fin_s = P.cnt["sync"]
        P.raw("sync", lambda engine, v=fin_s: engine.wait_ge(P.sems["sync"], v))
        fin_g = P.cnt["gpsimd_dma"]
        P.raw("gpsimd", lambda engine, v=fin_g: engine.wait_ge(P.sems["gpsimd_dma"], v))

    if stage == 3:
        bmat_s = sb3("bmat_s", [128, NCHI * DS], F32)
        cmat_s = sb3("cmat_s", [128, NCHI * DS], F32)
        dvec_s = sb3("dvec_s", [128, NCHI], F32)
        chain_s = sb3("chain_s", [128, NCORES], F32)
        cb_col = sb3("cb_col", [128, NCHI * DS], F32)
        hinit_s = sb3("hinit_s", [128, NCHI * DS], F32)
        tmpc2 = sb3("tmpc2", [128, NCHI * DS], F32)
        tmpc3 = sb3("tmpc3", [128, NCHI * DS], F32)
        maskA = sb3("maskA", [128, NCHI * DS], mybir.dt.int32)
        fallj = [sb3(f"fallj{i}", [128, NCHI * DS], F32) for i in range(2)]
        h2b = [sb3(f"h2b{i}", [128, tc], BF16) for i in range(2)]
        diagb = [sb3(f"diag{i}", [128, 128], BF16) for i in range(2)]
        outw_b = sb3("outw_b", [128, NCHI, D], BF16)
        wstgB = [sb3(f"wstgB{i}", [128, D], F32) for i in range(2)]
        zld = [sb3(f"zld{i}", [128, tc], BF16) for i in range(2)]
        x0rot = [sb3(f"x0rot{i}", [128, D], F32) for i in range(2)]
        ytmp = sb3("ytmp", [128, 512], F32)
        ao_f = sb3("ao_f", [128, D], F32)

        t_bm = P.op("sync", lambda s: s.dma_start(out=bmat_s[:], in_=bmat_col[:, :]), dma=True)
        t_cm = P.op("sync", lambda s: s.dma_start(out=cmat_s[:], in_=cmat_col[:, :]), dma=True)
        t_ch = P.op("sync", lambda s: s.dma_start(out=chain_s[:], in_=chain_e[:, :]), dma=True)
        t_wsc = P.op("sync", lambda s: s.dma_start(out=wsc_s[:], in_=wsc_in[:, :]), dma=True)
        t_outb = P.op("sync", lambda s: s.dma_start(out=outb_s[:], in_=outb_rep[:, :]), dma=True)
        t_b2s = P.op("sync", lambda s: s.dma_start(out=b2_s[:], in_=b2_rep[:, :]), dma=True)
        t_dvecl = P.op("sync", lambda s: s.dma_start(out=dvec_s[:], in_=dvec_col[:, :]), dma=True)
        t_dvec = P.op("vector", lambda v: v.tensor_scalar(
            out=dvec_s[:], in0=dvec_s[:], scalar1=1.0, scalar2=None, op0=Alu.add),
            deps=[t_dvecl])

        t_den = P.op("vector", lambda v: v.tensor_scalar(
            out=tmpc2[:], in0=tmpc1[:], scalar1=-1.0, scalar2=1e-8, op0=Alu.mult, op1=Alu.add),
            deps=[te_] + tg)
        t_num = P.op("vector", lambda v: v.tensor_scalar(
            out=tmpc3[:], in0=a_col[:], scalar1=1.0, scalar2=None, op0=Alu.subtract),
            deps=[t_a])
        t_rcp = P.op("vector", lambda v: v.reciprocal(out=tmpc2[:], in_=tmpc2[:]),
                     deps=[t_den])
        t_rat = P.op("vector", lambda v: v.tensor_tensor(
            out=tmpc3[:], in0=tmpc3[:], in1=tmpc2[:], op=Alu.mult), deps=[t_num, t_rcp])
        t_absg = P.op("scalar", lambda s: s.activation(out=tmpc2[:], in_=g_col[:], func=Act.Abs),
                      deps=[t_rat, t_a])
        t_mask = P.op("vector", lambda v: v.tensor_scalar(
            out=maskA[:], in0=tmpc2[:], scalar1=1e-6, scalar2=None, op0=Alu.is_lt),
            deps=[t_absg])
        tsel = []
        for m in range(NCHI):
            tsel.append(P.op("vector", lambda v, m=m: v.select(
                out=tmpc1[:, m * DS:(m + 1) * DS], mask=maskA[:, m * DS:(m + 1) * DS],
                on_true=dtav_s[:, m:m + 1].broadcast_to([128, DS]),
                on_false=tmpc3[:, m * DS:(m + 1) * DS], add_drain=True),
                deps=[t_mask, t_rat, t_den]))
        t_bbar = P.op("vector", lambda v: v.tensor_tensor(
            out=tmpc3[:], in0=tmpc1[:], in1=bmat_s[:], op=Alu.mult),
            deps=tsel + [t_bm])
        t_cb = P.op("vector", lambda v: v.tensor_tensor(
            out=cb_col[:], in0=tmpc3[:], in1=cmat_s[:], op=Alu.mult),
            deps=[t_bbar, t_cm])

        hcur = None
        fprev = [None, None]
        for j in range(NCORES):
            tfl = P.op("sync", lambda s, j=j: s.dma_start(
                out=fallj[j % 2][:], in_=f_all[j, :, :]),
                deps=[fprev[j % 2]], dma=True)
            tj1 = P.op("vector", lambda v, j=j: v.tensor_scalar(
                out=tmpc1[:], in0=g_col[:], scalar1=chain_s[:, j:j + 1], scalar2=None,
                op0=Alu.mult), deps=[t_cb, t_ch, hcur])
            tj2 = P.op("scalar", lambda s: s.activation(out=tmpc2[:], in_=tmpc1[:], func=Act.Exp),
                       deps=[tj1])
            tj3 = P.op("vector", lambda v, j=j: v.tensor_tensor(
                out=tmpc2[:], in0=tmpc2[:], in1=fallj[j % 2][:], op=Alu.mult), deps=[tj2, tfl])
            fprev[j % 2] = tj3
            if j == 0:
                hcur = P.op("vector", lambda v: v.tensor_copy(out=hinit_s[:], in_=tmpc2[:]),
                            deps=[tj3])
            else:
                hcur = P.op("vector", lambda v: v.tensor_tensor(
                    out=hinit_s[:], in0=hinit_s[:], in1=tmpc2[:], op=Alu.add),
                    deps=[tj3, hcur])
        t_hinit = hcur

        tow = []
        prevw = [None, None]
        for m in range(NCHI):
            i = m % 2
            tl = P.op("sync", lambda s, m=m, i=i: s.dma_start(
                out=wstgB[i][:], in_=outw_t[m * 128:(m + 1) * 128, :]),
                deps=[prevw[i]], dma=True)
            tw = P.op("vector", lambda v, m=m, i=i: v.tensor_copy(
                out=outw_b[:, m, :], in_=wstgB[i][:]), deps=[tl])
            prevw[i] = tw
            tow.append(tw)

        sc_hist = []
        ygate_tok = []
        zconsume = [None, None]
        for m in range(NCHI):
            i = m % 2
            tzl = P.op("sync", lambda s, m=m, i=i: s.dma_start(
                out=zld[i][:], in_=z_in[m, :, :]),
                deps=[zconsume[i]], dma=True)
            mm = None
            for s_ in range(DS):
                col = m * DS + s_
                hb = h2b[(m * DS + s_) % 2]
                scdeps = [t_hinit, t_a, t_xsl[m]]
                if len(sc_hist) >= 2:
                    scdeps.append(sc_hist[-2][1])
                sc = P.op("vector", lambda v, m=m, col=col, hb=hb: v.tensor_tensor_scan(
                    out=hb[:], data0=a_col[:, col:col + 1].broadcast_to([128, tc]),
                    data1=xs_sb2[:, m, :], initial=hinit_s[:, col:col + 1],
                    op0=Alu.mult, op1=Alu.add), deps=scdeps)
                dgdeps = [t_cb, t_identb]
                if len(sc_hist) >= 2:
                    dgdeps.append(sc_hist[-2][1])
                dg = P.op("vector", lambda v, col=col, idx=m * DS + s_: v.tensor_scalar(
                    out=diagb[idx % 2][:], in0=ident_b[:], scalar1=cb_col[:, col:col + 1],
                    scalar2=None, op0=Alu.mult), deps=dgdeps)
                for ckn in range(nct):
                    mmdeps = [sc, dg]
                    if s_ == 0 and ygate_tok:
                        mmdeps.append(ygate_tok[-1])
                    mm = P.op("tensor", lambda t, s_=s_, ckn=ckn, hb=hb, idx=m * DS + s_:
                              t.matmul(out=pYb[ckn][:, :], lhsT=diagb[idx % 2][:],
                                       rhs=hb[:, ckn * 512:(ckn + 1) * 512],
                                       start=(s_ == 0), stop=(s_ == DS - 1)), deps=mmdeps)
                sc_hist.append((sc, mm))
            last = None
            for ckn in range(nct):
                tv1 = P.op("vector", lambda v, ckn=ckn, i=i: v.tensor_tensor(
                    out=ytmp[:], in0=pYb[ckn][:, :], in1=zld[i][:, ckn * 512:(ckn + 1) * 512],
                    op=Alu.mult), deps=[mm, tzl, last])
                last = P.op("vector", lambda v, m=m, ckn=ckn: v.tensor_scalar(
                    out=xs_sb2[:, m, ckn * 512:(ckn + 1) * 512], in0=ytmp[:],
                    scalar1=dvec_s[:, m:m + 1], scalar2=None, op0=Alu.mult),
                    deps=[tv1, t_dvec])
            zconsume[i] = last
            ygate_tok.append(last)

        x0prev = [None, None]
        t_x1 = []
        for tt2 in range(ntok):
            i = tt2 % 2
            stp, st6, mvv = stripsR[i], stat6R[i], mvR[i]
            tx0 = P.op("sync", lambda s, stp=stp, st6=st6, mvv=mvv, tt2=tt2, i=i: s.dma_start(
                out=x0rot[i][:], in_=x_res[tt2 * 128:(tt2 + 1) * 128, :]),
                deps=[x0prev[i]], dma=True)
            mm = None
            for ckn in range(2):
                pp = pA if ckn == 0 else pB
                for m in range(NCHI):
                    mmdeps = [ygate_tok[m], tow[m]]
                    if t_x1:
                        mmdeps.append(t_x1[-1])
                    mm = P.op("tensor", lambda t, stp=stp, st6=st6, mvv=mvv, tt2=tt2, m=m, ckn=ckn, pp=pp: t.matmul(
                        out=pp[:, :], lhsT=xs_sb2[:, m, tt2 * 128:(tt2 + 1) * 128],
                        rhs=outw_b[:, m, ckn * 512:(ckn + 1) * 512],
                        start=(m == 0), stop=(m == NCHI - 1)), deps=mmdeps)
            e1 = P.op("scalar", lambda s: s.activation(out=ao_f[:, 0:512], in_=pA[:, :],
                                                       func=Act.Identity), deps=[mm])
            e2 = P.op("scalar", lambda s: s.activation(out=ao_f[:, 512:1024], in_=pB[:, :],
                                                       func=Act.Identity), deps=[mm])
            tb = P.op("vector", lambda v, stp=stp, st6=st6, mvv=mvv: v.tensor_tensor(
                out=ao_f[:], in0=ao_f[:], in1=outb_s[:], op=Alu.add),
                deps=[e1, e2, t_outb])
            b1 = P.op("vector", lambda v, stp=stp, st6=st6, mvv=mvv: v.bn_stats(out=st6[:, 0:6], in_=ao_f[:, 0:512]),
                      deps=[tb])
            b2t = P.op("vector", lambda v, stp=stp, st6=st6, mvv=mvv: v.bn_stats(out=st6[:, 6:12], in_=ao_f[:, 512:1024]),
                       deps=[tb])
            ag = P.op("vector", lambda v, stp=stp, st6=st6, mvv=mvv: v.bn_aggr(out=mvv[:], in_=st6[:, 0:12]),
                      deps=[b1, b2t])
            ve = P.op("vector", lambda v, stp=stp, st6=st6, mvv=mvv: v.tensor_scalar(
                out=stp[:, 6:7], in0=mvv[:, 1:2], scalar1=EPS, scalar2=None, op0=Alu.add),
                deps=[ag])
            rs0 = P.op("scalar", lambda s, stp=stp, st6=st6, mvv=mvv: s.activation(out=stp[:, 7:8], in_=stp[:, 6:7],
                                                        func=Act.Sqrt), deps=[ve])
            rs = P.op("vector", lambda v, stp=stp, st6=st6, mvv=mvv: v.reciprocal(out=stp[:, 7:8], in_=stp[:, 7:8]),
                      deps=[rs0])
            nrm = P.op("vector", lambda v, stp=stp, st6=st6, mvv=mvv: v.tensor_scalar(
                out=ao_f[:], in0=ao_f[:], scalar1=mvv[:, 0:1], scalar2=stp[:, 7:8],
                op0=Alu.subtract, op1=Alu.mult), deps=[rs])
            res = P.op("vector", lambda v, stp=stp, st6=st6, mvv=mvv, i=i: v.tensor_tensor(
                out=ao_f[:], in0=ao_f[:], in1=x0rot[i][:], op=Alu.add), deps=[nrm, tx0])
            x0prev[i] = res
            t_x1.append(P.op("sync", lambda s, stp=stp, st6=st6, mvv=mvv, tt2=tt2: s.dma_start(
                out=x1_spill[tt2, :, :], in_=ao_f[:]), deps=[res], dma=True))

        esA3.close()
        barF = [(e, P.cnt[e]) for e in P.keys if P.cnt[e] > 0]
        w1_b = sb("w1_b", [128, NDT, HF], BF16)
        w2_b = sb("w2_b", [128, NH, D], BF16)
        b1_s = sb("b1_s", [128, HF], F32)
        x1t = [sb(f"x1t{i}", [128, D], F32) for i in range(2)]
        xq1 = sb("xq1", [128, D], BF16)
        xq1T = sb("xq1T", [128, NDT, 128], BF16)
        h_f = sb("h_f", [128, HF], F32)
        hq = sb("hq", [128, HF], BF16)
        hqT = sb("hqT", [128, NH, 128], BF16)
        o_f = sb("o_f", [128, D], F32)
        xln = sb("xln", [128, D], F32)
        qtmp = sb("qtmp", [128, HF // 8], F32)

        t_b1 = P.op("sync", lambda s: s.dma_start(out=b1_s[:], in_=b1_rep[:, :]),
                    deps=barF, dma=True)
        tw1 = []
        for kd in range(NDT):
            tw1.append(P.op("sync", lambda s, kd=kd: s.dma_start(
                out=w1_b[:, kd, :], in_=w1q_in[kd, :, :]), dma=True))
        tw2 = []
        for kh in range(NH):
            tw2.append(P.op("sync", lambda s, kh=kh: s.dma_start(
                out=w2_b[:, kh, :], in_=w2q_in[kh, :, :]), dma=True))

        x1prev = [None, None]
        t_out = []
        pv = {"q7": None, "oev": None, "r7": None, "t2e": None, "ttrl": None}
        TS = [dict() for _ in range(ntok)]

        def phaseA(tt2):
            i = tt2 % 2
            stp, st6, mvv = stripsR[i], stat6R[i], mvR[i]
            S = TS[tt2]
            S["i"] = i
            txl = P.op("sync", lambda s, tt2=tt2, i=i: s.dma_start(
                out=x1t[i][:], in_=x1_spill[tt2, :, :]),
                deps=[t_x1[tt2], x1prev[i]], dma=True)
            f1 = P.op("vector", lambda v, st6=st6, i=i: v.bn_stats(out=st6[:, 0:6], in_=x1t[i][:, 0:512]),
                      deps=[txl])
            f2 = P.op("vector", lambda v, st6=st6, i=i: v.bn_stats(out=st6[:, 6:12], in_=x1t[i][:, 512:1024]),
                      deps=[txl])
            f3 = P.op("vector", lambda v, st6=st6, mvv=mvv: v.bn_aggr(out=mvv[:], in_=st6[:, 0:12]),
                      deps=[f1, f2])
            f4 = P.op("vector", lambda v, stp=stp, mvv=mvv: v.tensor_scalar(
                out=stp[:, 6:7], in0=mvv[:, 1:2], scalar1=EPS, scalar2=None, op0=Alu.add),
                deps=[f3])
            f5 = P.op("scalar", lambda s, stp=stp: s.activation(out=stp[:, 7:8], in_=stp[:, 6:7],
                                                       func=Act.Sqrt), deps=[f4])
            f6 = P.op("vector", lambda v, stp=stp: v.reciprocal(out=stp[:, 7:8], in_=stp[:, 7:8]),
                      deps=[f5])
            f7 = P.op("vector", lambda v, stp=stp, mvv=mvv, i=i: v.tensor_scalar(
                out=xln[:], in0=x1t[i][:], scalar1=mvv[:, 0:1], scalar2=stp[:, 7:8],
                op0=Alu.subtract, op1=Alu.mult), deps=[f6, pv["q7"]])
            q1 = P.op("vector", lambda v, stp=stp: v.tensor_reduce(
                out=stp[:, 0:1], in_=xln[:], axis=AxX, op=Alu.max,
                apply_absolute_value=True), deps=[f7])
            q2 = P.op("vector", lambda v, stp=stp: v.tensor_scalar(
                out=stp[:, 0:1], in0=stp[:, 0:1], scalar1=EPS, scalar2=None,
                op0=Alu.max), deps=[q1])
            q3 = P.op("vector", lambda v, stp=stp: v.reciprocal(out=stp[:, 1:2], in_=stp[:, 0:1]),
                      deps=[q2])
            q3b = P.op("vector", lambda v, stp=stp: v.tensor_scalar(
                out=stp[:, 1:2], in0=stp[:, 1:2], scalar1=QB, scalar2=None,
                op0=Alu.mult), deps=[q3])
            q7s = []
            for qq in range(2):
                q4 = P.op("vector", lambda v, stp=stp, qq=qq: v.tensor_scalar(
                    out=qtmp[:], in0=xln[:, qq * 512:(qq + 1) * 512],
                    scalar1=stp[:, 1:2], scalar2=MAGIC,
                    op0=Alu.mult, op1=Alu.add),
                    deps=[q3b] + q7s + ([pv["r7"]] if qq == 0 else []))
                q7s.append(P.op("vector", lambda v, qq=qq: v.tensor_scalar(
                    out=xq1[:, qq * 512:(qq + 1) * 512], in0=qtmp[:],
                    scalar1=-MAGIC, scalar2=None, op0=Alu.add),
                    deps=[q4] + ([pv["ttrl"]] if tt2 > 0 and qq == 0 else [])))
            S["q7"] = q7s[-1]
            S["q2"] = q2
            pv["q7"] = q7s[-1]

        def phaseB(tt2):
            S = TS[tt2]
            i = S["i"]
            stp, st6, mvv = stripsR[i], stat6R[i], mvR[i]
            q7 = S["q7"]
            ttr = []
            for kd in range(NDT):
                tp1_ = P.op("tensor", lambda t, kd=kd: t.transpose(
                    out=pT[kd % 2][:, :],
                    in_=xq1[:, kd * 128:(kd + 1) * 128], identity=ident_b[:]),
                    deps=[q7, t_identb] + ([ttr[-2]] if len(ttr) >= 2 else [pv["t2e"]]))
                ttr.append(P.op("scalar", lambda s, kd=kd: s.activation(
                    out=xq1T[:, kd, :], in_=pT[kd % 2][:, :], func=Act.Identity),
                    deps=[tp1_]))
            pv["ttrl"] = ttr[-1]
            sc1 = P.op("vector", lambda v, stp=stp: v.tensor_scalar(
                out=stp[:, 2:3], in0=stp[:, 0:1], scalar1=1.0 / QB,
                scalar2=wsc_s[:, 0:1], op0=Alu.mult, op1=Alu.mult),
                deps=[S["q2"], t_wsc])
            hev = []
            for ckn in range(8):
                pp = pA if ckn % 2 == 0 else pB
                mm = None
                for kd in range(NDT):
                    mmdeps = [ttr[kd], tw1[kd]]
                    if len(hev) >= 2:
                        mmdeps.append(hev[-2])
                    if ckn == 0 and kd == 0:
                        mmdeps.append(pv["oev"])
                    mm = P.op("tensor", lambda t, kd=kd, ckn=ckn, pp=pp: t.matmul(
                        out=pp[:, :], lhsT=xq1T[:, kd, :],
                        rhs=w1_b[:, kd, ckn * 512:(ckn + 1) * 512],
                        start=(kd == 0), stop=(kd == NDT - 1)), deps=mmdeps)
                hev.append(P.op("scalar", lambda s, stp=stp, ckn=ckn, pp=pp: s.activation(
                    out=h_f[:, ckn * 512:(ckn + 1) * 512], in_=pp[:, :], func=Act.Identity,
                    scale=stp[:, 2:3]),
                    deps=[mm, sc1] + ([pv["r7"]] if ckn < 2 else [])))
            hb1 = P.op("vector", lambda v: v.tensor_tensor(
                out=h_f[:], in0=h_f[:], in1=b1_s[:], op=Alu.add), deps=hev + [t_b1])
            hg = P.op("scalar", lambda s: s.activation(out=h_f[:], in_=h_f[:], func=Act.Gelu),
                      deps=[hb1])
            r1 = P.op("vector", lambda v, stp=stp: v.tensor_reduce(
                out=stp[:, 3:4], in_=h_f[:], axis=AxX, op=Alu.max,
                apply_absolute_value=True), deps=[hg])
            r2 = P.op("vector", lambda v, stp=stp: v.tensor_scalar(
                out=stp[:, 3:4], in0=stp[:, 3:4], scalar1=EPS, scalar2=None,
                op0=Alu.max), deps=[r1])
            r3 = P.op("vector", lambda v, stp=stp: v.reciprocal(out=stp[:, 4:5], in_=stp[:, 3:4]),
                      deps=[r2])
            r3b = P.op("vector", lambda v, stp=stp: v.tensor_scalar(
                out=stp[:, 4:5], in0=stp[:, 4:5], scalar1=QB, scalar2=None,
                op0=Alu.mult), deps=[r3])
            r7list = []
            for hh in range(8):
                lo, hi = hh * (HF // 8), (hh + 1) * (HF // 8)
                r4 = P.op("vector", lambda v, stp=stp, lo=lo, hi=hi: v.tensor_scalar(
                    out=qtmp[:], in0=h_f[:, lo:hi], scalar1=stp[:, 4:5], scalar2=MAGIC,
                    op0=Alu.mult, op1=Alu.add), deps=[r3b] + r7list)
                r7list.append(P.op("vector", lambda v, lo=lo, hi=hi: v.tensor_scalar(
                    out=hq[:, lo:hi], in0=qtmp[:], scalar1=-MAGIC, scalar2=None,
                    op0=Alu.add), deps=[r4] + ([pv["t2e"]] if hh == 0 else [])))
            S["r7"] = r7list[-1]
            S["r2"] = r2
            pv["r7"] = r7list[-1]

        def phaseC(tt2):
            S = TS[tt2]
            i = S["i"]
            stp, st6, mvv = stripsR[i], stat6R[i], mvR[i]
            r7 = S["r7"]
            ttr2 = []
            for kh in range(NH):
                tp2_ = P.op("tensor", lambda t, kh=kh: t.transpose(
                    out=pT[kh % 2][:, :],
                    in_=hq[:, kh * 128:(kh + 1) * 128], identity=ident_b[:]),
                    deps=[r7] + ([ttr2[-2]] if len(ttr2) >= 2 else []))
                ttr2.append(P.op("scalar", lambda s, kh=kh: s.activation(
                    out=hqT[:, kh, :], in_=pT[kh % 2][:, :], func=Act.Identity),
                    deps=[tp2_]))
            pv["t2e"] = ttr2[-1]
            sc2 = P.op("vector", lambda v, stp=stp: v.tensor_scalar(
                out=stp[:, 5:6], in0=stp[:, 3:4], scalar1=1.0 / QB,
                scalar2=wsc_s[:, 2:3], op0=Alu.mult, op1=Alu.mult),
                deps=[S["r2"], t_wsc])
            oev = []
            for ckn in range(2):
                pp = pA if ckn == 0 else pB
                mm = None
                for kh in range(NH):
                    mmdeps = [ttr2[kh], tw2[kh]]
                    mm = P.op("tensor", lambda t, kh=kh, ckn=ckn, pp=pp: t.matmul(
                        out=pp[:, :], lhsT=hqT[:, kh, :],
                        rhs=w2_b[:, kh, ckn * 512:(ckn + 1) * 512],
                        start=(kh == 0), stop=(kh == NH - 1)), deps=mmdeps)
                oev.append(P.op("scalar", lambda s, stp=stp, ckn=ckn, pp=pp: s.activation(
                    out=o_f[:, ckn * 512:(ckn + 1) * 512], in_=pp[:, :], func=Act.Identity,
                    scale=stp[:, 5:6]),
                    deps=[mm, sc2] + (t_out[-1:] if t_out else [])))
            ob = P.op("vector", lambda v: v.tensor_tensor(
                out=o_f[:], in0=o_f[:], in1=b2_s[:], op=Alu.add),
                deps=oev + [t_b2s])
            b1x = P.op("vector", lambda v, st6=st6: v.bn_stats(out=st6[:, 0:6], in_=o_f[:, 0:512]),
                       deps=[ob])
            b2x = P.op("vector", lambda v, st6=st6: v.bn_stats(out=st6[:, 6:12], in_=o_f[:, 512:1024]),
                       deps=[ob])
            agx = P.op("vector", lambda v, st6=st6, mvv=mvv: v.bn_aggr(out=mvv[:], in_=st6[:, 0:12]),
                       deps=[b1x, b2x])
            vex = P.op("vector", lambda v, stp=stp, mvv=mvv: v.tensor_scalar(
                out=stp[:, 6:7], in0=mvv[:, 1:2], scalar1=EPS, scalar2=None, op0=Alu.add),
                deps=[agx])
            rsx0 = P.op("scalar", lambda s, stp=stp: s.activation(out=stp[:, 7:8], in_=stp[:, 6:7],
                                                         func=Act.Sqrt), deps=[vex])
            rsx = P.op("vector", lambda v, stp=stp: v.reciprocal(out=stp[:, 7:8], in_=stp[:, 7:8]),
                       deps=[rsx0])
            nrmx = P.op("vector", lambda v, stp=stp, mvv=mvv: v.tensor_scalar(
                out=o_f[:], in0=o_f[:], scalar1=mvv[:, 0:1], scalar2=stp[:, 7:8],
                op0=Alu.subtract, op1=Alu.mult), deps=[rsx])
            resx = P.op("vector", lambda v, i=i: v.tensor_tensor(
                out=o_f[:], in0=o_f[:], in1=x1t[i][:], op=Alu.add), deps=[nrmx])
            x1prev[i] = resx
            pv["oev"] = oev[-1]
            t_out.append(P.op("sync", lambda s, tt2=tt2: s.dma_start(
                out=out_d[tt2 * 128:(tt2 + 1) * 128, :], in_=o_f[:]), deps=[resx], dma=True))

        for rnd in range(ntok + 2):
            if rnd >= 2:
                phaseC(rnd - 2)
            if 1 <= rnd <= ntok:
                phaseB(rnd - 1)
            if rnd < ntok:
                phaseA(rnd)

        fin_s = P.cnt["sync"]
        P.raw("sync", lambda engine, v=fin_s: engine.wait_ge(P.sems["sync"], v))

    P.flush()
    es.close()
    return nc


def prep_inputs(inputs, tc):
    x = np.asarray(inputs["L0_attn_x"], np.float32).reshape(-1, D)
    tp = ((tc + 3 + 127) // 128) * 128

    conv_w = np.asarray(inputs["L0_attn_conv_w"], np.float32).reshape(DI, 4)
    gate_w = np.asarray(inputs["L0_attn_gate_w"], np.float32).reshape(DI, 4)
    convw_all = np.concatenate([conv_w, gate_w], axis=0)
    convb_all = np.concatenate([np.asarray(inputs["L0_attn_conv_b"], np.float32),
                                np.asarray(inputs["L0_attn_gate_b"], np.float32)])

    rep = lambda v: np.ascontiguousarray(
        np.broadcast_to(np.asarray(v, np.float32)[None, :], (128, len(v))))
    Tpose = lambda a: np.ascontiguousarray(np.asarray(a, np.float32).T)

    c1 = {
        "ident": np.eye(128, dtype=np.float32),
        "ones_row": np.ones((1, 128), np.float32),
        "w_in_t": Tpose(inputs["L0_attn_in_proj_w"]),
        "ln_w_col": col_layout(np.asarray(inputs["L0_attn_ln_w"], np.float32)[:, None]),
        "ln_b_col": col_layout(np.asarray(inputs["L0_attn_ln_b"], np.float32)[:, None]),
        "in_b_col": col_layout(np.asarray(inputs["L0_attn_in_proj_b"], np.float32)[:, None]),
        "convw_col": col_layout(convw_all),
        "convb_col": col_layout(convb_all[:, None]),
        "aproj_t": Tpose(inputs["L0_attn_A_proj_w"]),
        "dtw_t": Tpose(inputs["L0_attn_dt_proj_w"]),
        "dtb_col": col_layout(np.asarray(inputs["L0_attn_dt_proj_b"], np.float32)[:, None]),
    }
    w1t_full = Tpose(inputs["L0_ffn_W1"])
    w2t_full = Tpose(inputs["L0_ffn_W2"])
    c3 = {
        "ident": c1["ident"],
        "alog_col": col_layout(np.asarray(inputs["L0_attn_A_log"], np.float32)),
        "bmat_col": col_layout(np.asarray(inputs["L0_attn_B_mat"], np.float32)),
        "cmat_col": col_layout(np.asarray(inputs["L0_attn_C_mat"], np.float32)),
        "dvec_col": col_layout(np.asarray(inputs["L0_attn_D_vec"], np.float32)[:, None]),
        "outw_t": Tpose(inputs["L0_attn_out_proj_w"]),
        "outb_rep": rep(np.asarray(inputs["L0_attn_out_proj_b"], np.float32)),
        "b1_rep": rep(np.asarray(inputs["L0_ffn_b1"], np.float32)),
        "b2_rep": rep(np.asarray(inputs["L0_ffn_b2"], np.float32)),
    }

    maps1, maps3 = [], []
    for k in range(NCORES):
        lo = k * tc
        xtok = np.zeros((tp, D), np.float32)
        if k == 0:
            xtok[3:3 + tc] = x[lo:lo + tc]
        else:
            xtok[0:3 + tc] = x[lo - 3:lo + tc]
        halo = np.ones((128, 4), np.float32)
        if k == 0:
            halo[:, 0:3] = 0.0
        ce = np.full((NCORES,), 1e9, np.float32)
        for j in range(k):
            ce[j] = float(tc * (k - 1 - j))
        m1 = dict(c1)
        m1["x_tok"] = xtok
        m1["halo_mask"] = halo
        m1["w1_sh"] = np.ascontiguousarray(w1t_full[k * 128:(k + 1) * 128])
        m1["w2_sh"] = np.ascontiguousarray(w2t_full[k * (HF // NCORES):(k + 1) * (HF // NCORES)])
        maps1.append(m1)
        m3 = dict(c3)
        m3["x_res"] = np.ascontiguousarray(x[lo:lo + tc])
        m3["chain_e"] = np.ascontiguousarray(np.broadcast_to(ce[None, :], (128, NCORES)))
        maps3.append(m3)
    return maps1, maps3


def kernel(**inputs):
    x = np.asarray(inputs["L0_attn_x"])
    T = x.reshape(-1, D).shape[0]
    tc = T // NCORES
    maps1, maps3 = prep_inputs(inputs, tc)
    cores = list(range(NCORES))

    nc1 = build_nc(tc, 1)
    r1 = run_bass_kernel_spmd(nc1, maps1, core_ids=cores).results

    dt_all = np.ascontiguousarray(np.stack([np.asarray(r["dtp_io"]) for r in r1]))
    ident = maps1[0]["ident"]
    alog = col_layout(np.asarray(inputs["L0_attn_A_log"], np.float32))
    wpart_all = np.ascontiguousarray(np.stack([np.asarray(r["wpart_io"]) for r in r1]))
    revramp = np.ascontiguousarray(np.broadcast_to(
        np.arange(tc - 1, -1, -1, dtype=np.float32)[None, :], (128, tc)))
    maps2 = [{"ident": ident, "alog_col": alog, "dt_all": dt_all,
              "xs_io": np.asarray(r1[i]["xs_io"]),
              "wpart_all": wpart_all, "ones_row": maps1[i]["ones_row"],
              "w1_sh": maps1[i]["w1_sh"], "w2_sh": maps1[i]["w2_sh"],
              "revramp": revramp}
             for i in range(NCORES)]
    nc2 = build_nc(tc, 2)
    r2 = run_bass_kernel_spmd(nc2, maps2, core_ids=cores).results

    f_all = np.ascontiguousarray(np.stack([np.asarray(r["f_io"]) for r in r2]))
    w1q_full = np.ascontiguousarray(np.concatenate([np.asarray(r["w1q_io"]) for r in r2], axis=0))
    w2q_full = np.ascontiguousarray(np.concatenate([np.asarray(r["w2q_io"]) for r in r2], axis=0))
    wsc0 = np.asarray(r2[0]["wsc_io"])
    for i in range(NCORES):
        maps3[i]["dt_all"] = dt_all
        maps3[i]["f_all"] = f_all
        maps3[i]["xs_io"] = np.asarray(r1[i]["xs_io"])
        maps3[i]["z_io"] = np.asarray(r1[i]["z_io"])
        maps3[i]["w1q_io"] = w1q_full
        maps3[i]["w2q_io"] = w2q_full
        maps3[i]["wsc_io"] = wsc0
    nc3 = build_nc(tc, 3)
    r3 = run_bass_kernel_spmd(nc3, maps3, core_ids=cores).results

    out = np.concatenate([np.asarray(r["out"]) for r in r3], axis=0)
    return out.reshape(x.shape).astype(np.float32)


# version 18
# speedup vs baseline: 1.0814x; 1.0814x over previous
import numpy as np
from contextlib import ExitStack

import concourse.bass as bass
import concourse.mybir as mybir
from concourse.bass_utils import run_bass_kernel_spmd

F32 = mybir.dt.float32
BF16 = mybir.dt.bfloat16
Alu = mybir.AluOpType
Act = mybir.ActivationFunctionType
AxX = mybir.AxisListType.X

D = 1024
DI = 2048
DXZ = 4096
HF = 4096
DS = 16
DTR = 64
NCORES = 8
EPS = 1e-5
MAGIC = float(1.5 * 2.0 ** 23)
QB = 127.0

NDT = D // 128
NCH = DXZ // 128
NCHI = DI // 128
NH = HF // 128

ENGS = ("sync", "tensor", "vector", "scalar", "gpsimd")


class Prog:

    DMA_KEYS = ("sync", "gpsimd_dma")

    def __init__(self, nc, blk, es):
        self.nc = nc
        self.blk = blk
        keys = list(ENGS) + ["gpsimd_dma"]
        self.keys = keys
        self.queues = {e: [] for e in ENGS}
        self.cnt = {e: 0 for e in keys}
        self.sems = {e: es.enter_context(nc.semaphore("sem_" + e)) for e in keys}
        self.hwm = {e: {f: 0 for f in keys} for e in keys}
        self.fence_val = {e: 0 for e in keys}

    @staticmethod
    def _issuer(key):
        return "gpsimd" if key == "gpsimd_dma" else key

    def _fence(self, e):
        if self.fence_val[e] < self.cnt[e]:
            v = self.cnt[e]
            sem = self.sems[e]
            self.queues[self._issuer(e)].append(
                lambda engine, sem=sem, v=v: engine.wait_ge(sem, v))
            self.fence_val[e] = v
            ik = self._issuer(e)
            self.hwm[ik][e] = max(self.hwm[ik][e], v)
        return self.fence_val[e]

    def op(self, eng, fn, deps=(), dma=False):
        key = eng
        if dma and eng == "gpsimd":
            key = "gpsimd_dma"
        amt = 16 if (dma or eng == "sync") else 1
        need = {}
        for d in deps:
            if d is None:
                continue
            e, c = d
            if e in self.DMA_KEYS:
                if self._issuer(e) == eng and c >= self.fence_val[e]:
                    c = self.cnt[e]
                    self.fence_val[e] = c
                elif c > self.fence_val[e]:
                    c = self._fence(e)
                else:
                    c = self.fence_val[e] if c > 0 else 0
            need[e] = max(need.get(e, 0), c)
        waits = []
        for e, c in need.items():
            if c > self.hwm[key][e]:
                waits.append((self.sems[e], c))
                self.hwm[key][e] = c
        self.cnt[key] += amt
        sem = self.sems[key]

        def emit(engine, waits=waits, fn=fn, sem=sem, amt=amt):
            for s, c in waits:
                engine.wait_ge(s, c)
            fn(engine).then_inc(sem, amt)

        self.queues[eng].append(emit)
        return (key, self.cnt[key])

    def raw(self, eng, fn):
        self.queues[eng].append(fn)

    def flush(self):
        for e in ENGS:
            q = self.queues[e]
            if q:
                getattr(self.blk, e)(lambda engine, q=q: [f(engine) for f in q])
            self.queues[e] = []


def col_layout(x):
    CH, k = x.shape
    nt = CH // 128
    return np.ascontiguousarray(x.reshape(nt, 128, k).transpose(1, 0, 2).reshape(128, nt * k))


def build_nc(tc, stage, s1cut=99):
    ntok = tc // 128
    tp = ((tc + 3 + 127) // 128) * 128
    ntokp = tp // 128
    ncw = (tp + 511) // 512
    nct = tc // 512
    ttot = float(tc * NCORES)

    nc = bass.Bass("TRN2", target_bir_lowering=False)

    dget = lambda n, s: nc.dram_tensor(n, s, F32, kind="ExternalInput").ap()
    dgetb = lambda n, s: nc.dram_tensor(n, s, BF16, kind="ExternalInput").ap()
    doutf = lambda n, s: nc.dram_tensor(n, s, F32, kind="ExternalOutput").ap()
    doutb = lambda n, s: nc.dram_tensor(n, s, BF16, kind="ExternalOutput").ap()

    ident_in = dget("ident", [128, 128])
    if stage == 1:
        x_tok = dget("x_tok", [tp, D])
        w_in_t = dget("w_in_t", [D, DXZ])
        ln_w_col = dget("ln_w_col", [128, NDT])
        ln_b_col = dget("ln_b_col", [128, NDT])
        in_b_col = dget("in_b_col", [128, NCH])
        convw_col = dget("convw_col", [128, NCH * 4])
        convb_col = dget("convb_col", [128, NCH])
        aproj_t = dget("aproj_t", [DI, DTR])
        dtw_t = dget("dtw_t", [DTR, DI])
        dtb_col = dget("dtb_col", [128, NCHI])
        w1_sh = dget("w1_sh", [128, HF])
        w2_sh = dget("w2_sh", [HF // NCORES, D])
        halo_mask = dget("halo_mask", [128, 4])
        ones_row_in = dget("ones_row", [1, 128])
        xs_io = doutb("xs_io", [NCHI, 128, tc])
        z_io = doutb("z_io", [NCHI, 128, tc])
        dtp_io = doutf("dtp_io", [128, NCHI])
        wpart_io = doutf("wpart_io", [128, 8])
    if stage in (2, 3):
        alog_col = dget("alog_col", [128, NCHI * DS])
        dt_all = dget("dt_all", [NCORES, 128, NCHI])
        xs_in = dgetb("xs_io", [NCHI, 128, tc])
    if stage == 2:
        f_out = doutf("f_io", [128, NCHI * DS])
        wpart_all = dget("wpart_all", [NCORES, 128, 8])
        revramp = dget("revramp", [128, tc])
        w1_sh = dget("w1_sh", [128, HF])
        w2_sh = dget("w2_sh", [HF // NCORES, D])
        ones_row_in = dget("ones_row", [1, 128])
        w1q_io = doutb("w1q_io", [1, 128, HF])
        w2q_io = doutb("w2q_io", [NH // NCORES, 128, D])
        wsc_io = doutf("wsc_io", [128, 8])
    if stage == 3:
        bmat_col = dget("bmat_col", [128, NCHI * DS])
        cmat_col = dget("cmat_col", [128, NCHI * DS])
        dvec_col = dget("dvec_col", [128, NCHI])
        chain_e = dget("chain_e", [128, NCORES])
        f_all = dget("f_all", [NCORES, 128, NCHI * DS])
        z_in = dgetb("z_io", [NCHI, 128, tc])
        x_res = dget("x_res", [tc, D])
        outw_t = dget("outw_t", [DI, D])
        outb_rep = dget("outb_rep", [128, D])
        b1_rep = dget("b1_rep", [128, HF])
        b2_rep = dget("b2_rep", [128, D])
        w1q_in = dgetb("w1q_io", [NDT, 128, HF])
        w2q_in = dgetb("w2q_io", [NH, 128, D])
        wsc_in = dget("wsc_io", [128, 8])
        out_d = doutf("out", [tc, D])
        x1_spill = nc.dram_tensor("x1_spill", [ntok, 128, D], F32, kind="Internal").ap()

    es = ExitStack()
    sb = lambda n, s, dt: es.enter_context(nc.sbuf_tensor(n, s, dt))
    psf = lambda n, s, dt=F32: es.enter_context(nc.psum_tensor(n, s, dt))

    blk = es.enter_context(nc.Block())
    P = Prog(nc, blk, es)

    ident_f = sb("ident_f", [128, 128], F32)
    ident_b = sb("ident_b", [128, 128], BF16)
    stat6 = sb("stat6", [128, 12], F32)
    mv_s = sb("mv_s", [128, 2], F32)
    mv2_s = sb("mv2_s", [128, 2], F32)
    strips = sb("strips", [128, 8], F32)
    dtav_s = sb("dtav_s", [128, NCHI], F32)
    if stage == 3:
        wsc_s = sb("wsc_s", [128, 8], F32)
        outb_s = sb("outb_s", [128, D], F32)
        b2_s = sb("b2_s", [128, D], F32)
        stripsR = [sb(f"stripsR{i}", [128, 8], F32) for i in range(2)]
        stat6R = [sb(f"stat6R{i}", [128, 12], F32) for i in range(2)]
        mvR = [sb(f"mvR{i}", [128, 2], F32) for i in range(2)]

    pA = psf("pA", [128, 512])
    pB = psf("pB", [128, 512])
    pT = [psf(f"pT{i}", [128, 128], BF16) for i in range(2)]
    if stage == 1:
        pS = psf("pS", [128, 16])
    if stage == 3:
        pYb = [psf(f"pY{i}", [128, 512]) for i in range(4)]

    t_identl = P.op("sync", lambda s: s.dma_start(out=ident_f[:], in_=ident_in[:, :]), dma=True)
    t_identb = P.op("vector", lambda v: v.tensor_copy(out=ident_b[:], in_=ident_f[:]),
                    deps=[t_identl])

    if stage == 1:
        onesrow_s = sb("onesrow_s", [1, 128], F32)
        onescol_s = sb("onescol_s", [128, 1], F32)
        lnw_s = sb("lnw_s", [128, NDT], F32)
        lnb_s = sb("lnb_s", [128, NDT], F32)
        lnb_b = sb("lnb_b", [128, NDT], BF16)
        inb_s = sb("inb_s", [128, NCH], F32)
        cvec_s = sb("cvec_s", [128, NCH], F32)
        convw_s = sb("convw_s", [128, NCH * 4], F32)
        convb_s = sb("convb_s", [128, NCH], F32)
        dtb_s = sb("dtb_s", [128, NCHI], F32)
        halo_s = sb("halo_s", [128, 4], F32)
        dts_strip = sb("dts_strip", [128, NCHI * 8], F32)
        wsc_s = sb("wsc_s", [128, 8], F32)
        wstrip = sb("wstrip", [128, 128], F32)
        xs_sb = sb("xs_sb", [128, NCHI, tc], BF16)
        diagb = [sb(f"diag{i}", [128, 128], BF16) for i in range(2)]
        esW = ExitStack()
        sbW = lambda n, s, dt: esW.enter_context(nc.sbuf_tensor(n, s, dt))
        w_half = sbW("w_half", [128, NDT, DI], BF16)
        v_T = sbW("v_T", [128, NDT, tp], BF16)
        wstg = [sbW(f"wstg{i}", [128, 1024], F32) for i in range(2)]
        xtile = [sbW("xtl0", [128, D], F32)] * 2
        u_f = sbW("u_f", [128, D], F32)
        ub_b = sbW("ub_b", [128, D], BF16)
        xz_t = sbW("xz_t", [128, tp + 8], BF16)
        zrot = sbW("zrot", [128, tc], BF16)
        ai_b = sbW("ai_b", [128, tc], BF16)
        aprj_b = sbW("aprj_b", [128, NCHI, DTR], BF16)
        dtw_b = sbW("dtw_b", [128, DI], BF16)
        junkA = sbW("junkA", [128, 1024], F32)

        t_small = {}
        for key, dst, srcd in (
            ("onesrow", onesrow_s, ones_row_in), ("lnw", lnw_s, ln_w_col),
            ("lnb", lnb_s, ln_b_col), ("inb", inb_s, in_b_col),
            ("convw", convw_s, convw_col), ("convb", convb_s, convb_col),
            ("dtb", dtb_s, dtb_col), ("halo", halo_s, halo_mask),
        ):
            t_small[key] = P.op("sync", lambda s, dst=dst, srcd=srcd:
                                s.dma_start(out=dst[:], in_=srcd[:, :]), dma=True)
        t_ones = P.op("vector", lambda v: v.memset(onescol_s[:], 1.0))
        t_wsc0 = P.op("vector", lambda v: v.memset(wsc_s[:], 0.0))

        if s1cut < 1:
            fin_s = P.cnt["sync"]
            P.raw("sync", lambda engine, v=fin_s: engine.wait_ge(P.sems["sync"], v))
            fin_v = P.cnt["vector"]
            P.raw("vector", lambda engine, v=fin_v: engine.wait_ge(P.sems["vector"], v))
            P.flush()
            es.close()
            return nc
        t_lnbb = P.op("vector", lambda v: v.tensor_copy(out=lnb_b[:], in_=lnb_s[:]),
                      deps=[t_small["lnb"]])

        tr_tok = []
        trp_last = None
        uo_prev = None
        for tt in range(ntokp):
            i = tt % 2
            xt = xtile[i]
            tld = P.op("sync", lambda s, tt=tt, i=i: s.dma_start(
                out=xtile[i][:], in_=x_tok[tt * 128:(tt + 1) * 128, :]),
                deps=[uo_prev], dma=True)
            b1 = P.op("vector", lambda v, xt=xt: v.bn_stats(out=stat6[:, 0:6], in_=xt[:, 0:512]),
                      deps=[tld])
            b2t = P.op("vector", lambda v, xt=xt: v.bn_stats(out=stat6[:, 6:12], in_=xt[:, 512:1024]),
                       deps=[tld])
            ag = P.op("vector", lambda v: v.bn_aggr(out=mv_s[:], in_=stat6[:, 0:12]),
                      deps=[b1, b2t])
            ve = P.op("vector", lambda v: v.tensor_scalar(
                out=strips[:, 6:7], in0=mv_s[:, 1:2], scalar1=EPS, scalar2=None, op0=Alu.add),
                deps=[ag])
            rs0 = P.op("scalar", lambda s: s.activation(out=strips[:, 7:8], in_=strips[:, 6:7],
                                                        func=Act.Sqrt), deps=[ve])
            rs = P.op("vector", lambda v: v.reciprocal(out=strips[:, 7:8], in_=strips[:, 7:8]),
                      deps=[rs0])
            uo = P.op("vector", lambda v, xt=xt: v.tensor_scalar(
                out=u_f[:], in0=xt[:], scalar1=mv_s[:, 0:1], scalar2=strips[:, 7:8],
                op0=Alu.subtract, op1=Alu.mult), deps=[rs])
            uo_prev = uo
            b3 = P.op("vector", lambda v: v.bn_stats(out=stat6[:, 0:6], in_=u_f[:, 0:512]),
                      deps=[uo])
            b4 = P.op("vector", lambda v: v.bn_stats(out=stat6[:, 6:12], in_=u_f[:, 512:1024]),
                      deps=[uo])
            ag2 = P.op("vector", lambda v: v.bn_aggr(out=mv2_s[:], in_=stat6[:, 0:12]),
                       deps=[b3, b4])
            ve2 = P.op("vector", lambda v: v.tensor_scalar(
                out=strips[:, 6:7], in0=mv2_s[:, 1:2], scalar1=EPS, scalar2=None, op0=Alu.add),
                deps=[ag2])
            rs20 = P.op("scalar", lambda s: s.activation(out=strips[:, 7:8], in_=strips[:, 6:7],
                                                         func=Act.Sqrt), deps=[ve2])
            rs2 = P.op("vector", lambda v: v.reciprocal(out=strips[:, 7:8], in_=strips[:, 7:8]),
                       deps=[rs20])
            ub = P.op("vector", lambda v: v.tensor_scalar(
                out=ub_b[:], in0=u_f[:], scalar1=mv2_s[:, 0:1], scalar2=strips[:, 7:8],
                op0=Alu.subtract, op1=Alu.mult),
                deps=[rs2] + ([trp_last] if tr_tok else []))
            for kd in range(NDT):
                trp = P.op("tensor", lambda t, kd=kd: t.transpose(
                    out=pT[kd % 2][:, :],
                    in_=ub_b[:, kd * 128:(kd + 1) * 128], identity=ident_b[:]),
                    deps=[ub, t_identb] + ([tr_tok[-2]] if len(tr_tok) >= 2 else []))
                trp_last = trp
                cv = P.op("vector", lambda v, tt=tt, kd=kd: v.tensor_copy(
                    out=v_T[:, kd, tt * 128:(tt + 1) * 128],
                    in_=pT[kd % 2][:, :]), deps=[trp])
                tr_tok.append(cv)
        t_vT = tr_tok[-1]

        if s1cut < 2:
            fin_s = P.cnt["sync"]
            P.raw("sync", lambda engine, v=fin_s: engine.wait_ge(P.sems["sync"], v))
            fin_v = P.cnt["vector"]
            P.raw("vector", lambda engine, v=fin_v: engine.wait_ge(P.sems["vector"], v))
            P.flush()
            es.close()
            return nc

        def attn_half(half, dep_start):
            ch0 = half * NCHI
            tcast = []
            prevw = [dep_start, dep_start]
            for kd in range(NDT):
                tcst = None
                for hh in range(2):
                    i = (2 * kd + hh) % 2
                    tl = P.op("sync", lambda s, kd=kd, i=i, hh=hh, half=half: s.dma_start(
                        out=wstg[i][:], in_=w_in_t[kd * 128:(kd + 1) * 128,
                                                   half * DI + hh * 1024:half * DI + (hh + 1) * 1024]),
                        deps=[prevw[i]], dma=True)
                    tc2_ = P.op("vector", lambda v, kd=kd, i=i, hh=hh: v.tensor_copy(
                        out=w_half[:, kd, hh * 1024:(hh + 1) * 1024], in_=wstg[i][:]), deps=[tl])
                    prevw[i] = tc2_
                    tcst = tc2_ if tcst is None else (tc2_ if tc2_[1] > tcst[1] else tcst)
                tcast.append(tcst)
            tcv = []
            for m in range(NCHI):
                mm = None
                for kd in range(NDT):
                    mm = P.op("tensor", lambda t, m=m, kd=kd: t.matmul(
                        out=pS[:, 8:9], lhsT=w_half[:, kd, m * 128:(m + 1) * 128],
                        rhs=lnb_b[:, kd:kd + 1], start=(kd == 0), stop=(kd == NDT - 1)),
                        deps=[tcast[kd], t_lnbb] + (tcv[-1:] if tcv else []))
                tcv.append(P.op("vector", lambda v, m=m, ch0=ch0: v.tensor_tensor(
                    out=cvec_s[:, ch0 + m:ch0 + m + 1], in0=pS[:, 8:9],
                    in1=inb_s[:, ch0 + m:ch0 + m + 1], op=Alu.add),
                    deps=[mm, t_small["inb"]]))
            tsc = []
            for kd in range(NDT):
                tsc.append(P.op("vector", lambda v, kd=kd: v.tensor_scalar(
                    out=w_half[:, kd, :], in0=w_half[:, kd, :], scalar1=lnw_s[:, kd:kd + 1],
                    scalar2=None, op0=Alu.mult),
                    deps=[tcast[kd], tcv[-1], t_small["lnw"]]))

            out_tok = []
            prev_conv_last = None
            for m in range(NCHI):
                chg = ch0 + m
                evs = []
                for ckn in range(ncw):
                    c0 = ckn * 512
                    cw = min(512, tp - c0)
                    pp = pA if ckn % 2 == 0 else pB
                    mm = None
                    for kd in range(NDT):
                        ipdeps = [tsc[kd], t_vT]
                        if len(evs) >= 2:
                            ipdeps.append(evs[-2])
                        elif out_tok:
                            ipdeps.append(out_tok[-1])
                        mm = P.op("tensor", lambda t, m=m, kd=kd, c0=c0, cw=cw, pp=pp: t.matmul(
                            out=pp[:, 0:cw], lhsT=w_half[:, kd, m * 128:(m + 1) * 128],
                            rhs=v_T[:, kd, c0:c0 + cw], start=(kd == 0), stop=(kd == NDT - 1)),
                            deps=ipdeps)
                    evs.append(P.op("vector", lambda v, c0=c0, cw=cw, pp=pp, chg=chg: v.tensor_scalar(
                        out=xz_t[:, c0:c0 + cw], in0=pp[:, 0:cw],
                        scalar1=cvec_s[:, chg:chg + 1], scalar2=None, op0=Alu.add),
                        deps=[mm] + ([prev_conv_last] if ckn == 0 and prev_conv_last else [])))
                hm = P.op("vector", lambda v: v.tensor_tensor(
                    out=xz_t[:, 0:3], in0=xz_t[:, 0:3], in1=halo_s[:, 0:3], op=Alu.mult),
                    deps=[evs[0], t_small["halo"]])
                last_silu = None
                conv_mms = []
                for ckn in range(nct):
                    c0 = ckn * 512
                    pp = pA if ckn % 2 == 0 else pB
                    mm = None
                    for j in range(4):
                        idx = ckn * 4 + j
                        djdeps = [t_identb, t_small["convw"], evs[-1], hm]
                        if len(conv_mms) >= 2:
                            djdeps.append(conv_mms[-2])
                        dj = P.op("vector", lambda v, j=j, chg=chg, idx=idx: v.tensor_scalar(
                            out=diagb[idx % 2][:], in0=ident_b[:],
                            scalar1=convw_s[:, chg * 4 + j:chg * 4 + j + 1],
                            scalar2=None, op0=Alu.mult), deps=djdeps)
                        mmdeps = [dj, evs[-1], hm]
                        if last_silu is not None and j == 0:
                            mmdeps.append(last_silu)
                        mm = P.op("tensor", lambda t, j=j, c0=c0, pp=pp, idx=idx: t.matmul(
                            out=pp[:, 0:512], lhsT=diagb[idx % 2][:],
                            rhs=xz_t[:, c0 + j:c0 + j + 512], start=(j == 0), stop=(j == 3)),
                            deps=mmdeps)
                        conv_mms.append(mm)
                    prev_conv_last = mm
                    if half == 0:
                        last_silu = P.op("scalar", lambda s, m=m, c0=c0, pp=pp, chg=chg:
                                         s.activation(out=xs_sb[:, m, c0:c0 + 512], in_=pp[:, 0:512],
                                                      func=Act.Silu, bias=convb_s[:, chg:chg + 1],
                                                      scale=1.0), deps=[mm, t_small["convb"]])
                    else:
                        last_silu = P.op("scalar", lambda s, c0=c0, pp=pp, chg=chg:
                                         s.activation(out=zrot[:, c0:c0 + 512], in_=pp[:, 0:512],
                                                      func=Act.Silu, bias=convb_s[:, chg:chg + 1],
                                                      scale=1.0), deps=[mm, t_small["convb"]])
                if half == 1:
                    sp = P.op("sync", lambda s, m=m: s.dma_start(
                        out=z_io[m, :, :], in_=zrot[:]), deps=[last_silu], dma=True)
                    out_tok.append(sp)
                else:
                    out_tok.append(last_silu)
            return out_tok

        t_xs = attn_half(0, t_vT)

        if s1cut < 3:
            fin_s = P.cnt["sync"]
            P.raw("sync", lambda engine, v=fin_s: engine.wait_ge(P.sems["sync"], v))
            fin_a = P.cnt["scalar"]
            P.raw("scalar", lambda engine, v=fin_a: engine.wait_ge(P.sems["scalar"], v))
            P.flush()
            es.close()
            return nc

        t_xsout = []
        for m in range(NCHI):
            t_xsout.append(P.op("sync", lambda s, m=m: s.dma_start(
                out=xs_io[m, :, :], in_=xs_sb[:, m, :]), deps=[t_xs[m]], dma=True))

        tap = []
        prev = None
        for m in range(NCHI):
            tl = P.op("sync", lambda s, m=m: s.dma_start(
                out=wstg[0][:, 0:DTR], in_=aproj_t[m * 128:(m + 1) * 128, :]),
                deps=[prev] if prev else [t_xs[-1]], dma=True)
            prev = P.op("vector", lambda v, m=m: v.tensor_copy(
                out=aprj_b[:, m, :], in_=wstg[0][:, 0:DTR]), deps=[tl])
            tap.append(prev)
        for hh in range(2):
            tdw_l = P.op("sync", lambda s, hh=hh: s.dma_start(
                out=wstg[1][0:DTR, :], in_=dtw_t[:, hh * 1024:(hh + 1) * 1024]),
                deps=[prev], dma=True)
            prev = P.op("vector", lambda v, hh=hh: v.tensor_copy(
                out=dtw_b[0:DTR, hh * 1024:(hh + 1) * 1024], in_=wstg[1][0:DTR, :]),
                deps=[tdw_l])
        tdw = prev

        tai = []
        for ckn in range(nct):
            c0 = ckn * 512
            mm = None
            for m in range(NCHI):
                mm = P.op("tensor", lambda t, m=m, c0=c0: t.matmul(
                    out=pA[0:DTR, 0:512], lhsT=aprj_b[:, m, :], rhs=xs_sb[:, m, c0:c0 + 512],
                    start=(m == 0), stop=(m == NCHI - 1)),
                    deps=[tap[m], t_xs[m]] + (tai[-1:] if tai else []))
            tai.append(P.op("vector", lambda v, c0=c0: v.tensor_copy(
                out=ai_b[0:DTR, c0:c0 + 512], in_=pA[0:DTR, 0:512]), deps=[mm]))

        tdt = []
        for m in range(NCHI):
            for pair in range(max(1, nct // 2)):
                texp = []
                for sub in range(min(2, nct)):
                    ckn = pair * 2 + sub
                    c0 = ckn * 512
                    pp = pA if ckn % 2 == 0 else pB
                    mm = P.op("tensor", lambda t, m=m, c0=c0, pp=pp: t.matmul(
                        out=pp[:, 0:512], lhsT=dtw_b[0:DTR, m * 128:(m + 1) * 128],
                        rhs=ai_b[0:DTR, c0:c0 + 512], start=True, stop=True),
                        deps=[tdw, tai[ckn], tai[-1]] + (tdt[-1:] if tdt else []))
                    texp.append(P.op("scalar", lambda s, m=m, sub=sub, pp=pp: s.activation(
                        out=junkA[:, sub * 512:(sub + 1) * 512], in_=pp[:, 0:512], func=Act.Exp,
                        bias=dtb_s[:, m:m + 1], scale=1.0), deps=[mm, t_small["dtb"]]))
                for sub in range(min(2, nct)):
                    ckn = pair * 2 + sub
                    tdt.append(P.op("scalar", lambda s, m=m, ckn=ckn, sub=sub: s.activation(
                        out=u_f[:, 0:512], in_=junkA[:, sub * 512:(sub + 1) * 512], func=Act.Ln,
                        bias=1.0, scale=1.0,
                        accum_out=dts_strip[:, m * 8 + ckn:m * 8 + ckn + 1]),
                        deps=[texp[sub]]))
        t_dtred = P.op("vector", lambda v: v.tensor_reduce(
            out=dtav_s.ap().rearrange("p (m o) -> p m o", o=1),
            in_=dts_strip.ap().rearrange("p (m c) -> p m c", c=8)[:, :, 0:nct],
            axis=AxX, op=Alu.add), deps=tdt)
        P.op("sync", lambda s: s.dma_start(out=dtp_io[:, :], in_=dtav_s[:]),
             deps=[t_dtred], dma=True)

        if s1cut < 4:
            fin_s = P.cnt["sync"]
            P.raw("sync", lambda engine, v=fin_s: engine.wait_ge(P.sems["sync"], v))
            P.flush()
            es.close()
            return nc

        t_z = attn_half(1, t_dtred)

        if s1cut < 5:
            fin_s = P.cnt["sync"]
            P.raw("sync", lambda engine, v=fin_s: engine.wait_ge(P.sems["sync"], v))
            P.flush()
            es.close()
            return nc

        esW.close()
        barW = [(e, P.cnt[e]) for e in P.keys if P.cnt[e] > 0]
        NWB = 4
        wqst = [sb(f"wqst{i}", [128, 1024], F32) for i in range(NWB)]

        wq_abs_tok = []
        prevq = [None] * NWB
        first_wq = True
        kq = 0
        for wid, (w_dram, n_tiles) in enumerate(((w1_sh, 1), (w2_sh, 4))):
            colw = w_dram.shape[1]
            nck = max(1, colw // 1024)
            ckw = min(colw, 1024)
            for t in range(n_tiles):
                for ck in range(nck):
                    i = kq % NWB
                    tl = P.op("gpsimd", lambda g, st=wqst[i], w=w_dram, t=t, ck=ck, ckw=ckw:
                              g.dma_start(out=st[:, 0:ckw],
                                          in_=w[t * 128:(t + 1) * 128, ck * ckw:(ck + 1) * ckw]),
                              deps=[prevq[i]] + (barW if first_wq else []), dma=True)
                    first_wq = False
                    col = wid * 4 + t * nck + ck
                    ta = P.op("vector", lambda v, st=wqst[i], ckw=ckw, col=col:
                              v.tensor_reduce(out=wstrip[:, col:col + 1], in_=st[:, 0:ckw],
                                              axis=AxX, op=Alu.add,
                                              apply_absolute_value=True),
                              deps=[tl])
                    prevq[i] = ta
                    wq_abs_tok.append(ta)
                    kq += 1
        P.op("sync", lambda s: s.dma_start(out=wpart_io[:, :], in_=wstrip[:, 0:8]),
             deps=wq_abs_tok, dma=True)

        fin_s = P.cnt["sync"]
        P.raw("sync", lambda engine, v=fin_s: engine.wait_ge(P.sems["sync"], v))
        fin_g = P.cnt["gpsimd_dma"]
        P.raw("gpsimd", lambda engine, v=fin_g: engine.wait_ge(P.sems["gpsimd_dma"], v))

    if stage in (2, 3):
        esA3 = ExitStack()
        sb3 = lambda n, s, dt: esA3.enter_context(nc.sbuf_tensor(n, s, dt))
        alog_s = sb3("alog_s", [128, NCHI * DS], F32)
        g_col = sb3("g_col", [128, NCHI * DS], F32)
        a_col = sb3("a_col", [128, NCHI * DS], F32)
        tmpc1 = sb3("tmpc1", [128, NCHI * DS], F32)
        dtall_s = sb3("dtall_s", [128, NCORES, NCHI], F32)
        xs_sb2 = sb3("xs_sb2", [128, NCHI, tc], BF16)

        t_alog = P.op("sync", lambda s: s.dma_start(out=alog_s[:], in_=alog_col[:, :]), dma=True)
        t_dta = P.op("sync", lambda s: s.dma_start(
            out=dtall_s[:, :, :],
            in_=dt_all.rearrange("j p m -> p j m")), dma=True)
        t_xsl = []
        for m in range(NCHI):
            t_xsl.append(P.op("sync", lambda s, m=m: s.dma_start(
                out=xs_sb2[:, m, :], in_=xs_in[m, :, :]), dma=True))

        t_dts = P.op("vector", lambda v: v.tensor_copy(
            out=dtav_s[:], in_=dtall_s[:, 0, :]), deps=[t_dta])
        for j in range(1, NCORES):
            t_dts = P.op("vector", lambda v, j=j: v.tensor_tensor(
                out=dtav_s[:], in0=dtav_s[:], in1=dtall_s[:, j, :], op=Alu.add),
                deps=[t_dts, t_dta])
        t_dtav = P.op("vector", lambda v: v.tensor_scalar(
            out=dtav_s[:], in0=dtav_s[:], scalar1=1.0 / ttot, scalar2=None, op0=Alu.mult),
            deps=[t_dts])

        te_ = P.op("scalar", lambda s: s.activation(out=tmpc1[:], in_=alog_s[:], func=Act.Exp),
                   deps=[t_alog])
        tg = []
        for m in range(NCHI):
            tg.append(P.op("vector", lambda v, m=m: v.tensor_scalar(
                out=g_col[:, m * DS:(m + 1) * DS], in0=tmpc1[:, m * DS:(m + 1) * DS],
                scalar1=dtav_s[:, m:m + 1], scalar2=-1.0, op0=Alu.mult, op1=Alu.mult),
                deps=[te_, t_dtav]))
        t_a = P.op("scalar", lambda s: s.activation(out=a_col[:], in_=g_col[:], func=Act.Exp),
                   deps=tg)

    if stage == 2:
        apb = [sb3(f"apb{i}", [128, tc], F32) for i in range(2)]
        mtb = [sb3(f"mtb{i}", [128, tc], F32) for i in range(2)]
        dump2 = sb3("dump2", [128, tc], F32)
        ramp_s = sb3("ramp_s", [128, tc], F32)
        floc_s = sb3("floc_s", [128, NCHI * DS], F32)
        t_rmp = P.op("sync", lambda s: s.dma_start(
            out=ramp_s[:], in_=revramp[:, :]), dma=True)
        prevM = [None, None]
        prevAcc = [None, None]
        last_dve = None
        last_act = None
        etok = [None] * (NCHI * DS)

        def emit_e(col):
            ib = col % 2
            m = col // DS
            etok[col] = P.op("scalar", lambda s, ib=ib, col=col: s.activation(
                out=apb[ib][:], in_=ramp_s[:], func=Act.Exp,
                scale=g_col[:, col:col + 1]),
                deps=[t_rmp, tg[m], prevM[ib]])

        emit_e(0)
        for m in range(NCHI):
            dve_path = (m % 3 == 0)
            for s_ in range(DS):
                col = m * DS + s_
                ib = col % 2
                if col + 1 < NCHI * DS:
                    emit_e(col + 1)
                mu = P.op("vector", lambda v, ib=ib, m=m: v.tensor_tensor(
                    out=mtb[ib][:], in0=apb[ib][:], in1=xs_sb2[:, m, :], op=Alu.mult),
                    deps=[etok[col], t_xsl[m], prevAcc[ib]])
                prevM[ib] = mu
                if dve_path:
                    acc = P.op("vector", lambda v, ib=ib, col=col: v.tensor_reduce(
                        out=floc_s[:, col:col + 1], in_=mtb[ib][:], axis=AxX,
                        op=Alu.add), deps=[mu, last_dve])
                    last_dve = acc
                else:
                    acc = P.op("scalar", lambda s, ib=ib, col=col: s.activation(
                        out=dump2[:], in_=mtb[ib][:], func=Act.Identity,
                        accum_out=floc_s[:, col:col + 1]), deps=[mu, last_act])
                    last_act = acc
                prevAcc[ib] = acc
        P.op("sync", lambda s: s.dma_start(out=f_out[:, :], in_=floc_s[:]),
             deps=[last_dve, last_act], dma=True)

        wp_s = sb3("wp_s", [128, NCORES, 8], F32)
        wstr2 = sb3("wstr2", [128, 8], F32)
        str8 = sb3("str8", [128, 8], F32)
        wsc2 = sb3("wsc2", [128, 8], F32)
        ones_c = sb3("ones_c", [128, 1], F32)
        ones_r = sb3("ones_r", [1, 128], F32)
        wq2st = [sb3(f"wq2st{i}", [128, 1024], F32) for i in range(2)]
        wq2tmp = sb3("wq2tmp", [128, 1024], F32)
        wq2sg = sb3("wq2sg", [128, 1024], F32)
        wq2out = [sb3(f"wq2out{i}", [128, 1024], BF16) for i in range(2)]
        pS2 = psf("pS2", [128, 8])

        t_wp = P.op("sync", lambda s: s.dma_start(
            out=wp_s[:, :, :], in_=wpart_all.rearrange("j p m -> p j m")), dma=True)
        t_or = P.op("sync", lambda s: s.dma_start(out=ones_r[:, :], in_=ones_row_in[:, :]), dma=True)
        t_oc = P.op("vector", lambda v: v.memset(ones_c[:], 1.0))
        t_w0 = P.op("vector", lambda v: v.memset(wsc2[:], 0.0))
        tsum = P.op("vector", lambda v: v.tensor_copy(out=wstr2[:], in_=wp_s[:, 0, :]),
                    deps=[t_wp])
        for j in range(1, NCORES):
            tsum = P.op("vector", lambda v, j=j: v.tensor_tensor(
                out=wstr2[:], in0=wstr2[:], in1=wp_s[:, j, :], op=Alu.add),
                deps=[tsum, t_wp])
        wq_scale_tok = [None, None]
        for wid in range(2):
            nel = float(D * HF)
            tr = P.op("vector", lambda v, wid=wid: v.tensor_reduce(
                out=str8[:, wid:wid + 1], in_=wstr2[:, wid * 4: wid * 4 + 4],
                axis=AxX, op=Alu.add), deps=[tsum])
            tm = P.op("tensor", lambda t, wid=wid: t.matmul(
                out=pS2[0:1, wid:wid + 1], lhsT=ones_c[:], rhs=str8[:, wid:wid + 1],
                start=True, stop=True), deps=[tr, t_oc])
            ts1 = P.op("vector", lambda v, wid=wid, nel=nel: v.tensor_scalar(
                out=str8[0:1, 2 + wid:3 + wid], in0=pS2[0:1, wid:wid + 1],
                scalar1=1.0 / nel, scalar2=EPS, op0=Alu.mult, op1=Alu.max), deps=[tm])
            ts2 = P.op("vector", lambda v, wid=wid: v.reciprocal(
                out=str8[0:1, 4 + wid:5 + wid], in_=str8[0:1, 2 + wid:3 + wid]),
                deps=[ts1])
            tb1 = P.op("tensor", lambda t, wid=wid: t.matmul(
                out=pS2[:, 2 + 2 * wid:3 + 2 * wid], lhsT=ones_r[0:1, :],
                rhs=str8[0:1, 2 + wid:3 + wid], start=True, stop=True),
                deps=[ts1, t_or])
            tb2 = P.op("tensor", lambda t, wid=wid: t.matmul(
                out=pS2[:, 3 + 2 * wid:4 + 2 * wid], lhsT=ones_r[0:1, :],
                rhs=str8[0:1, 4 + wid:5 + wid], start=True, stop=True), deps=[ts2])
            tc1 = P.op("vector", lambda v, wid=wid: v.tensor_copy(
                out=wsc2[:, 2 * wid:2 * wid + 1], in_=pS2[:, 2 + 2 * wid:3 + 2 * wid]),
                deps=[tb1, t_w0])
            tc2 = P.op("vector", lambda v, wid=wid: v.tensor_copy(
                out=wsc2[:, 2 * wid + 1:2 * wid + 2], in_=pS2[:, 3 + 2 * wid:4 + 2 * wid]),
                deps=[tb2])
            wq_scale_tok[wid] = (tc1, tc2)
        P.op("sync", lambda s: s.dma_start(out=wsc_io[:, :], in_=wsc2[:]),
             deps=[wq_scale_tok[0][1], wq_scale_tok[1][1]], dma=True)

        prev_t4 = None
        prev_sp = [None, None]
        prevld = [None, None]
        kq = 0
        for wid, (w_dram, n_tiles, spill) in enumerate(((w1_sh, 1, w1q_io), (w2_sh, 4, w2q_io))):
            colw = w_dram.shape[1]
            nck = max(1, colw // 1024)
            ckw = min(colw, 1024)
            for t in range(n_tiles):
                for ck in range(nck):
                    i = kq % 2
                    tl = P.op("gpsimd", lambda g, st=wq2st[i], w=w_dram, t=t, ck=ck, ckw=ckw:
                              g.dma_start(out=st[:, 0:ckw],
                                          in_=w[t * 128:(t + 1) * 128, ck * ckw:(ck + 1) * ckw]),
                              deps=[prevld[i]], dma=True)
                    t1 = P.op("vector", lambda v, st=wq2st[i], ckw=ckw, wid=wid:
                              v.tensor_scalar(out=wq2tmp[:, 0:ckw], in0=st[:, 0:ckw],
                                              scalar1=wsc2[:, 2 * wid + 1:2 * wid + 2],
                                              scalar2=None, op0=Alu.mult),
                              deps=[tl, wq_scale_tok[wid][1], prev_t4])
                    t2 = P.op("vector", lambda v, ckw=ckw:
                              v.tensor_scalar(out=wq2sg[:, 0:ckw], in0=wq2tmp[:, 0:ckw],
                                              scalar1=0.5, scalar2=None, op0=Alu.is_ge),
                              deps=[t1])
                    t3 = P.op("vector", lambda v, ckw=ckw:
                              v.tensor_scalar(out=wq2tmp[:, 0:ckw], in0=wq2tmp[:, 0:ckw],
                                              scalar1=-0.5, scalar2=None, op0=Alu.is_le),
                              deps=[t2])
                    t4 = P.op("vector", lambda v, ckw=ckw, ib=kq % 2:
                              v.tensor_tensor(out=wq2out[ib][:, 0:ckw], in0=wq2sg[:, 0:ckw],
                                              in1=wq2tmp[:, 0:ckw], op=Alu.subtract),
                              deps=[t3, prev_sp[kq % 2]])
                    t5 = P.op("gpsimd", lambda g, sp=spill, t=t, ck=ck, ckw=ckw, ib=kq % 2:
                              g.dma_start(out=sp[t, :, ck * ckw:(ck + 1) * ckw],
                                          in_=wq2out[ib][:, 0:ckw]), deps=[t4], dma=True)
                    prev_sp[kq % 2] = t5
                    prev_t4 = t4
                    prevld[i] = t4
                    kq += 1

        esA3.close()
        fin_s = P.cnt["sync"]
        P.raw("sync", lambda engine, v=fin_s: engine.wait_ge(P.sems["sync"], v))
        fin_g = P.cnt["gpsimd_dma"]
        P.raw("gpsimd", lambda engine, v=fin_g: engine.wait_ge(P.sems["gpsimd_dma"], v))

    if stage == 3:
        bmat_s = sb3("bmat_s", [128, NCHI * DS], F32)
        cmat_s = sb3("cmat_s", [128, NCHI * DS], F32)
        dvec_s = sb3("dvec_s", [128, NCHI], F32)
        chain_s = sb3("chain_s", [128, NCORES], F32)
        cb_col = sb3("cb_col", [128, NCHI * DS], F32)
        hinit_s = sb3("hinit_s", [128, NCHI * DS], F32)
        tmpc2 = sb3("tmpc2", [128, NCHI * DS], F32)
        tmpc3 = sb3("tmpc3", [128, NCHI * DS], F32)
        maskA = sb3("maskA", [128, NCHI * DS], mybir.dt.int32)
        fallj = [sb3(f"fallj{i}", [128, NCHI * DS], F32) for i in range(2)]
        h2b = [sb3(f"h2b{i}", [128, tc], BF16) for i in range(2)]
        diagb = [sb3(f"diag{i}", [128, 128], BF16) for i in range(2)]
        outw_b = sb3("outw_b", [128, NCHI, D], BF16)
        wstgB = [sb3(f"wstgB{i}", [128, D], F32) for i in range(2)]
        zld = [sb3(f"zld{i}", [128, tc], BF16) for i in range(2)]
        x0rot = [sb3(f"x0rot{i}", [128, D], F32) for i in range(2)]
        ytmp = sb3("ytmp", [128, 512], F32)
        ao_f = sb3("ao_f", [128, D], F32)

        t_bm = P.op("sync", lambda s: s.dma_start(out=bmat_s[:], in_=bmat_col[:, :]), dma=True)
        t_cm = P.op("sync", lambda s: s.dma_start(out=cmat_s[:], in_=cmat_col[:, :]), dma=True)
        t_ch = P.op("sync", lambda s: s.dma_start(out=chain_s[:], in_=chain_e[:, :]), dma=True)
        t_wsc = P.op("sync", lambda s: s.dma_start(out=wsc_s[:], in_=wsc_in[:, :]), dma=True)
        t_outb = P.op("sync", lambda s: s.dma_start(out=outb_s[:], in_=outb_rep[:, :]), dma=True)
        t_b2s = P.op("sync", lambda s: s.dma_start(out=b2_s[:], in_=b2_rep[:, :]), dma=True)
        t_dvecl = P.op("sync", lambda s: s.dma_start(out=dvec_s[:], in_=dvec_col[:, :]), dma=True)
        t_dvec = P.op("vector", lambda v: v.tensor_scalar(
            out=dvec_s[:], in0=dvec_s[:], scalar1=1.0, scalar2=None, op0=Alu.add),
            deps=[t_dvecl])

        t_den = P.op("vector", lambda v: v.tensor_scalar(
            out=tmpc2[:], in0=tmpc1[:], scalar1=-1.0, scalar2=1e-8, op0=Alu.mult, op1=Alu.add),
            deps=[te_] + tg)
        t_num = P.op("vector", lambda v: v.tensor_scalar(
            out=tmpc3[:], in0=a_col[:], scalar1=1.0, scalar2=None, op0=Alu.subtract),
            deps=[t_a])
        t_rcp = P.op("vector", lambda v: v.reciprocal(out=tmpc2[:], in_=tmpc2[:]),
                     deps=[t_den])
        t_rat = P.op("vector", lambda v: v.tensor_tensor(
            out=tmpc3[:], in0=tmpc3[:], in1=tmpc2[:], op=Alu.mult), deps=[t_num, t_rcp])
        t_absg = P.op("scalar", lambda s: s.activation(out=tmpc2[:], in_=g_col[:], func=Act.Abs),
                      deps=[t_rat, t_a])
        t_mask = P.op("vector", lambda v: v.tensor_scalar(
            out=maskA[:], in0=tmpc2[:], scalar1=1e-6, scalar2=None, op0=Alu.is_lt),
            deps=[t_absg])
        tsel = []
        for m in range(NCHI):
            tsel.append(P.op("vector", lambda v, m=m: v.select(
                out=tmpc1[:, m * DS:(m + 1) * DS], mask=maskA[:, m * DS:(m + 1) * DS],
                on_true=dtav_s[:, m:m + 1].broadcast_to([128, DS]),
                on_false=tmpc3[:, m * DS:(m + 1) * DS], add_drain=True),
                deps=[t_mask, t_rat, t_den]))
        t_bbar = P.op("vector", lambda v: v.tensor_tensor(
            out=tmpc3[:], in0=tmpc1[:], in1=bmat_s[:], op=Alu.mult),
            deps=tsel + [t_bm])
        t_cb = P.op("vector", lambda v: v.tensor_tensor(
            out=cb_col[:], in0=tmpc3[:], in1=cmat_s[:], op=Alu.mult),
            deps=[t_bbar, t_cm])

        hcur = None
        fprev = [None, None]
        for j in range(NCORES):
            tfl = P.op("sync", lambda s, j=j: s.dma_start(
                out=fallj[j % 2][:], in_=f_all[j, :, :]),
                deps=[fprev[j % 2]], dma=True)
            tj1 = P.op("vector", lambda v, j=j: v.tensor_scalar(
                out=tmpc1[:], in0=g_col[:], scalar1=chain_s[:, j:j + 1], scalar2=None,
                op0=Alu.mult), deps=[t_cb, t_ch, hcur])
            tj2 = P.op("scalar", lambda s: s.activation(out=tmpc2[:], in_=tmpc1[:], func=Act.Exp),
                       deps=[tj1])
            tj3 = P.op("vector", lambda v, j=j: v.tensor_tensor(
                out=tmpc2[:], in0=tmpc2[:], in1=fallj[j % 2][:], op=Alu.mult), deps=[tj2, tfl])
            fprev[j % 2] = tj3
            if j == 0:
                hcur = P.op("vector", lambda v: v.tensor_copy(out=hinit_s[:], in_=tmpc2[:]),
                            deps=[tj3])
            else:
                hcur = P.op("vector", lambda v: v.tensor_tensor(
                    out=hinit_s[:], in0=hinit_s[:], in1=tmpc2[:], op=Alu.add),
                    deps=[tj3, hcur])
        t_hinit = hcur

        tow = []
        prevw = [None, None]
        for m in range(NCHI):
            i = m % 2
            tl = P.op("sync", lambda s, m=m, i=i: s.dma_start(
                out=wstgB[i][:], in_=outw_t[m * 128:(m + 1) * 128, :]),
                deps=[prevw[i]], dma=True)
            tw = P.op("vector", lambda v, m=m, i=i: v.tensor_copy(
                out=outw_b[:, m, :], in_=wstgB[i][:]), deps=[tl])
            prevw[i] = tw
            tow.append(tw)

        sc_hist = []
        ygate_tok = []
        zconsume = [None, None]
        for m in range(NCHI):
            i = m % 2
            tzl = P.op("sync", lambda s, m=m, i=i: s.dma_start(
                out=zld[i][:], in_=z_in[m, :, :]),
                deps=[zconsume[i]], dma=True)
            mm = None
            for s_ in range(DS):
                col = m * DS + s_
                hb = h2b[(m * DS + s_) % 2]
                scdeps = [t_hinit, t_a, t_xsl[m]]
                if len(sc_hist) >= 2:
                    scdeps.append(sc_hist[-2][1])
                sc = P.op("vector", lambda v, m=m, col=col, hb=hb: v.tensor_tensor_scan(
                    out=hb[:], data0=a_col[:, col:col + 1].broadcast_to([128, tc]),
                    data1=xs_sb2[:, m, :], initial=hinit_s[:, col:col + 1],
                    op0=Alu.mult, op1=Alu.add), deps=scdeps)
                dgdeps = [t_cb, t_identb]
                if len(sc_hist) >= 2:
                    dgdeps.append(sc_hist[-2][1])
                dg = P.op("vector", lambda v, col=col, idx=m * DS + s_: v.tensor_scalar(
                    out=diagb[idx % 2][:], in0=ident_b[:], scalar1=cb_col[:, col:col + 1],
                    scalar2=None, op0=Alu.mult), deps=dgdeps)
                for ckn in range(nct):
                    mmdeps = [sc, dg]
                    if s_ == 0 and ygate_tok:
                        mmdeps.append(ygate_tok[-1])
                    mm = P.op("tensor", lambda t, s_=s_, ckn=ckn, hb=hb, idx=m * DS + s_:
                              t.matmul(out=pYb[ckn][:, :], lhsT=diagb[idx % 2][:],
                                       rhs=hb[:, ckn * 512:(ckn + 1) * 512],
                                       start=(s_ == 0), stop=(s_ == DS - 1)), deps=mmdeps)
                sc_hist.append((sc, mm))
            last = None
            for ckn in range(nct):
                tv1 = P.op("vector", lambda v, ckn=ckn, i=i: v.tensor_tensor(
                    out=ytmp[:], in0=pYb[ckn][:, :], in1=zld[i][:, ckn * 512:(ckn + 1) * 512],
                    op=Alu.mult), deps=[mm, tzl, last])
                last = P.op("vector", lambda v, m=m, ckn=ckn: v.tensor_scalar(
                    out=xs_sb2[:, m, ckn * 512:(ckn + 1) * 512], in0=ytmp[:],
                    scalar1=dvec_s[:, m:m + 1], scalar2=None, op0=Alu.mult),
                    deps=[tv1, t_dvec])
            zconsume[i] = last
            ygate_tok.append(last)

        x0prev = [None, None]
        t_x1 = []
        for tt2 in range(ntok):
            i = tt2 % 2
            stp, st6, mvv = stripsR[i], stat6R[i], mvR[i]
            tx0 = P.op("sync", lambda s, stp=stp, st6=st6, mvv=mvv, tt2=tt2, i=i: s.dma_start(
                out=x0rot[i][:], in_=x_res[tt2 * 128:(tt2 + 1) * 128, :]),
                deps=[x0prev[i]], dma=True)
            mm = None
            for ckn in range(2):
                pp = pA if ckn == 0 else pB
                for m in range(NCHI):
                    mmdeps = [ygate_tok[m], tow[m]]
                    if t_x1:
                        mmdeps.append(t_x1[-1])
                    mm = P.op("tensor", lambda t, stp=stp, st6=st6, mvv=mvv, tt2=tt2, m=m, ckn=ckn, pp=pp: t.matmul(
                        out=pp[:, :], lhsT=xs_sb2[:, m, tt2 * 128:(tt2 + 1) * 128],
                        rhs=outw_b[:, m, ckn * 512:(ckn + 1) * 512],
                        start=(m == 0), stop=(m == NCHI - 1)), deps=mmdeps)
            e1 = P.op("scalar", lambda s: s.activation(out=ao_f[:, 0:512], in_=pA[:, :],
                                                       func=Act.Identity), deps=[mm])
            e2 = P.op("scalar", lambda s: s.activation(out=ao_f[:, 512:1024], in_=pB[:, :],
                                                       func=Act.Identity), deps=[mm])
            tb = P.op("vector", lambda v, stp=stp, st6=st6, mvv=mvv: v.tensor_tensor(
                out=ao_f[:], in0=ao_f[:], in1=outb_s[:], op=Alu.add),
                deps=[e1, e2, t_outb])
            b1 = P.op("vector", lambda v, stp=stp, st6=st6, mvv=mvv: v.bn_stats(out=st6[:, 0:6], in_=ao_f[:, 0:512]),
                      deps=[tb])
            b2t = P.op("vector", lambda v, stp=stp, st6=st6, mvv=mvv: v.bn_stats(out=st6[:, 6:12], in_=ao_f[:, 512:1024]),
                       deps=[tb])
            ag = P.op("vector", lambda v, stp=stp, st6=st6, mvv=mvv: v.bn_aggr(out=mvv[:], in_=st6[:, 0:12]),
                      deps=[b1, b2t])
            ve = P.op("vector", lambda v, stp=stp, st6=st6, mvv=mvv: v.tensor_scalar(
                out=stp[:, 6:7], in0=mvv[:, 1:2], scalar1=EPS, scalar2=None, op0=Alu.add),
                deps=[ag])
            rs0 = P.op("scalar", lambda s, stp=stp, st6=st6, mvv=mvv: s.activation(out=stp[:, 7:8], in_=stp[:, 6:7],
                                                        func=Act.Sqrt), deps=[ve])
            rs = P.op("vector", lambda v, stp=stp, st6=st6, mvv=mvv: v.reciprocal(out=stp[:, 7:8], in_=stp[:, 7:8]),
                      deps=[rs0])
            nrm = P.op("vector", lambda v, stp=stp, st6=st6, mvv=mvv: v.tensor_scalar(
                out=ao_f[:], in0=ao_f[:], scalar1=mvv[:, 0:1], scalar2=stp[:, 7:8],
                op0=Alu.subtract, op1=Alu.mult), deps=[rs])
            res = P.op("vector", lambda v, stp=stp, st6=st6, mvv=mvv, i=i: v.tensor_tensor(
                out=ao_f[:], in0=ao_f[:], in1=x0rot[i][:], op=Alu.add), deps=[nrm, tx0])
            x0prev[i] = res
            t_x1.append(P.op("sync", lambda s, stp=stp, st6=st6, mvv=mvv, tt2=tt2: s.dma_start(
                out=x1_spill[tt2, :, :], in_=ao_f[:]), deps=[res], dma=True))

        esA3.close()
        barF = [(e, P.cnt[e]) for e in P.keys if P.cnt[e] > 0]
        w1_b = sb("w1_b", [128, NDT, HF], BF16)
        w2_b = sb("w2_b", [128, NH, D], BF16)
        b1_s = sb("b1_s", [128, HF], F32)
        x1t = [sb(f"x1t{i}", [128, D], F32) for i in range(2)]
        xq1 = sb("xq1", [128, D], BF16)
        xq1T = sb("xq1T", [128, NDT, 128], BF16)
        h_f = sb("h_f", [128, HF], F32)
        hq = sb("hq", [128, HF], BF16)
        hqT = sb("hqT", [128, NH, 128], BF16)
        o_f = sb("o_f", [128, D], F32)
        xln = sb("xln", [128, D], F32)
        qtmp = sb("qtmp", [128, HF // 8], F32)

        t_b1 = P.op("sync", lambda s: s.dma_start(out=b1_s[:], in_=b1_rep[:, :]),
                    deps=barF, dma=True)
        tw1 = []
        for kd in range(NDT):
            tw1.append(P.op("sync", lambda s, kd=kd: s.dma_start(
                out=w1_b[:, kd, :], in_=w1q_in[kd, :, :]), dma=True))
        tw2 = []
        for kh in range(NH):
            tw2.append(P.op("sync", lambda s, kh=kh: s.dma_start(
                out=w2_b[:, kh, :], in_=w2q_in[kh, :, :]), dma=True))

        x1prev = [None, None]
        t_out = []
        pv = {"q7": None, "oev": None, "r7": None, "t2e": None, "ttrl": None}
        TS = [dict() for _ in range(ntok)]

        def phaseA(tt2):
            i = tt2 % 2
            stp, st6, mvv = stripsR[i], stat6R[i], mvR[i]
            S = TS[tt2]
            S["i"] = i
            txl = P.op("sync", lambda s, tt2=tt2, i=i: s.dma_start(
                out=x1t[i][:], in_=x1_spill[tt2, :, :]),
                deps=[t_x1[tt2], x1prev[i]], dma=True)
            f1 = P.op("vector", lambda v, st6=st6, i=i: v.bn_stats(out=st6[:, 0:6], in_=x1t[i][:, 0:512]),
                      deps=[txl])
            f2 = P.op("vector", lambda v, st6=st6, i=i: v.bn_stats(out=st6[:, 6:12], in_=x1t[i][:, 512:1024]),
                      deps=[txl])
            f3 = P.op("vector", lambda v, st6=st6, mvv=mvv: v.bn_aggr(out=mvv[:], in_=st6[:, 0:12]),
                      deps=[f1, f2])
            f4 = P.op("vector", lambda v, stp=stp, mvv=mvv: v.tensor_scalar(
                out=stp[:, 6:7], in0=mvv[:, 1:2], scalar1=EPS, scalar2=None, op0=Alu.add),
                deps=[f3])
            f5 = P.op("scalar", lambda s, stp=stp: s.activation(out=stp[:, 7:8], in_=stp[:, 6:7],
                                                       func=Act.Sqrt), deps=[f4])
            f6 = P.op("vector", lambda v, stp=stp: v.reciprocal(out=stp[:, 7:8], in_=stp[:, 7:8]),
                      deps=[f5])
            f7 = P.op("vector", lambda v, stp=stp, mvv=mvv, i=i: v.tensor_scalar(
                out=xln[:], in0=x1t[i][:], scalar1=mvv[:, 0:1], scalar2=stp[:, 7:8],
                op0=Alu.subtract, op1=Alu.mult), deps=[f6, pv["q7"]])
            q1 = P.op("vector", lambda v, stp=stp: v.tensor_reduce(
                out=stp[:, 0:1], in_=xln[:], axis=AxX, op=Alu.max,
                apply_absolute_value=True), deps=[f7])
            q2 = P.op("vector", lambda v, stp=stp: v.tensor_scalar(
                out=stp[:, 0:1], in0=stp[:, 0:1], scalar1=EPS, scalar2=None,
                op0=Alu.max), deps=[q1])
            q3 = P.op("vector", lambda v, stp=stp: v.reciprocal(out=stp[:, 1:2], in_=stp[:, 0:1]),
                      deps=[q2])
            q3b = P.op("vector", lambda v, stp=stp: v.tensor_scalar(
                out=stp[:, 1:2], in0=stp[:, 1:2], scalar1=QB, scalar2=None,
                op0=Alu.mult), deps=[q3])
            q7s = []
            for qq in range(2):
                q4 = P.op("vector", lambda v, stp=stp, qq=qq: v.tensor_scalar(
                    out=qtmp[:], in0=xln[:, qq * 512:(qq + 1) * 512],
                    scalar1=stp[:, 1:2], scalar2=MAGIC,
                    op0=Alu.mult, op1=Alu.add),
                    deps=[q3b] + q7s + ([pv["r7"]] if qq == 0 else []))
                q7s.append(P.op("vector", lambda v, qq=qq: v.tensor_scalar(
                    out=xq1[:, qq * 512:(qq + 1) * 512], in0=qtmp[:],
                    scalar1=-MAGIC, scalar2=None, op0=Alu.add),
                    deps=[q4] + ([pv["ttrl"]] if tt2 > 0 and qq == 0 else [])))
            S["q7"] = q7s[-1]
            S["q2"] = q2
            pv["q7"] = q7s[-1]

        def phaseB(tt2):
            S = TS[tt2]
            i = S["i"]
            stp, st6, mvv = stripsR[i], stat6R[i], mvR[i]
            q7 = S["q7"]
            ttr = []
            for kd in range(NDT):
                tp1_ = P.op("tensor", lambda t, kd=kd: t.transpose(
                    out=pT[kd % 2][:, :],
                    in_=xq1[:, kd * 128:(kd + 1) * 128], identity=ident_b[:]),
                    deps=[q7, t_identb] + ([ttr[-2]] if len(ttr) >= 2 else [pv["t2e"]]))
                ttr.append(P.op("scalar", lambda s, kd=kd: s.activation(
                    out=xq1T[:, kd, :], in_=pT[kd % 2][:, :], func=Act.Identity),
                    deps=[tp1_]))
            pv["ttrl"] = ttr[-1]
            sc1 = P.op("vector", lambda v, stp=stp: v.tensor_scalar(
                out=stp[:, 2:3], in0=stp[:, 0:1], scalar1=1.0 / QB,
                scalar2=wsc_s[:, 0:1], op0=Alu.mult, op1=Alu.mult),
                deps=[S["q2"], t_wsc])
            hev = []
            for ckn in range(8):
                pp = pA if ckn % 2 == 0 else pB
                mm = None
                for kd in range(NDT):
                    mmdeps = [ttr[kd], tw1[kd]]
                    if len(hev) >= 2:
                        mmdeps.append(hev[-2])
                    if ckn == 0 and kd == 0:
                        mmdeps.append(pv["oev"])
                    mm = P.op("tensor", lambda t, kd=kd, ckn=ckn, pp=pp: t.matmul(
                        out=pp[:, :], lhsT=xq1T[:, kd, :],
                        rhs=w1_b[:, kd, ckn * 512:(ckn + 1) * 512],
                        start=(kd == 0), stop=(kd == NDT - 1)), deps=mmdeps)
                hev.append(P.op("scalar", lambda s, stp=stp, ckn=ckn, pp=pp: s.activation(
                    out=h_f[:, ckn * 512:(ckn + 1) * 512], in_=pp[:, :], func=Act.Identity,
                    scale=stp[:, 2:3]),
                    deps=[mm, sc1] + ([pv["r7"]] if ckn < 2 else [])))
            hb1 = P.op("vector", lambda v: v.tensor_tensor(
                out=h_f[:], in0=h_f[:], in1=b1_s[:], op=Alu.add), deps=hev + [t_b1])
            hg = P.op("scalar", lambda s: s.activation(out=h_f[:], in_=h_f[:], func=Act.Gelu),
                      deps=[hb1])
            r1 = P.op("vector", lambda v, stp=stp: v.tensor_reduce(
                out=stp[:, 3:4], in_=h_f[:], axis=AxX, op=Alu.max,
                apply_absolute_value=True), deps=[hg])
            r2 = P.op("vector", lambda v, stp=stp: v.tensor_scalar(
                out=stp[:, 3:4], in0=stp[:, 3:4], scalar1=EPS, scalar2=None,
                op0=Alu.max), deps=[r1])
            r3 = P.op("vector", lambda v, stp=stp: v.reciprocal(out=stp[:, 4:5], in_=stp[:, 3:4]),
                      deps=[r2])
            r3b = P.op("vector", lambda v, stp=stp: v.tensor_scalar(
                out=stp[:, 4:5], in0=stp[:, 4:5], scalar1=QB, scalar2=None,
                op0=Alu.mult), deps=[r3])
            r7list = []
            for hh in range(8):
                lo, hi = hh * (HF // 8), (hh + 1) * (HF // 8)
                r4 = P.op("vector", lambda v, stp=stp, lo=lo, hi=hi: v.tensor_scalar(
                    out=qtmp[:], in0=h_f[:, lo:hi], scalar1=stp[:, 4:5], scalar2=MAGIC,
                    op0=Alu.mult, op1=Alu.add), deps=[r3b] + r7list)
                r7list.append(P.op("vector", lambda v, lo=lo, hi=hi: v.tensor_scalar(
                    out=hq[:, lo:hi], in0=qtmp[:], scalar1=-MAGIC, scalar2=None,
                    op0=Alu.add), deps=[r4] + ([pv["t2e"]] if hh == 0 else [])))
            S["r7"] = r7list[-1]
            S["r2"] = r2
            pv["r7"] = r7list[-1]

        def phaseC(tt2):
            S = TS[tt2]
            i = S["i"]
            stp, st6, mvv = stripsR[i], stat6R[i], mvR[i]
            r7 = S["r7"]
            ttr2 = []
            for kh in range(NH):
                tp2_ = P.op("tensor", lambda t, kh=kh: t.transpose(
                    out=pT[kh % 2][:, :],
                    in_=hq[:, kh * 128:(kh + 1) * 128], identity=ident_b[:]),
                    deps=[r7] + ([ttr2[-2]] if len(ttr2) >= 2 else []))
                ttr2.append(P.op("scalar", lambda s, kh=kh: s.activation(
                    out=hqT[:, kh, :], in_=pT[kh % 2][:, :], func=Act.Identity),
                    deps=[tp2_]))
            pv["t2e"] = ttr2[-1]
            sc2 = P.op("vector", lambda v, stp=stp: v.tensor_scalar(
                out=stp[:, 5:6], in0=stp[:, 3:4], scalar1=1.0 / QB,
                scalar2=wsc_s[:, 2:3], op0=Alu.mult, op1=Alu.mult),
                deps=[S["r2"], t_wsc])
            oev = []
            for ckn in range(2):
                pp = pA if ckn == 0 else pB
                mm = None
                for kh in range(NH):
                    mmdeps = [ttr2[kh], tw2[kh]]
                    mm = P.op("tensor", lambda t, kh=kh, ckn=ckn, pp=pp: t.matmul(
                        out=pp[:, :], lhsT=hqT[:, kh, :],
                        rhs=w2_b[:, kh, ckn * 512:(ckn + 1) * 512],
                        start=(kh == 0), stop=(kh == NH - 1)), deps=mmdeps)
                oev.append(P.op("scalar", lambda s, stp=stp, ckn=ckn, pp=pp: s.activation(
                    out=o_f[:, ckn * 512:(ckn + 1) * 512], in_=pp[:, :], func=Act.Identity,
                    scale=stp[:, 5:6]),
                    deps=[mm, sc2] + (t_out[-1:] if t_out else [])))
            ob = P.op("vector", lambda v: v.tensor_tensor(
                out=o_f[:], in0=o_f[:], in1=b2_s[:], op=Alu.add),
                deps=oev + [t_b2s])
            b1x = P.op("vector", lambda v, st6=st6: v.bn_stats(out=st6[:, 0:6], in_=o_f[:, 0:512]),
                       deps=[ob])
            b2x = P.op("vector", lambda v, st6=st6: v.bn_stats(out=st6[:, 6:12], in_=o_f[:, 512:1024]),
                       deps=[ob])
            agx = P.op("vector", lambda v, st6=st6, mvv=mvv: v.bn_aggr(out=mvv[:], in_=st6[:, 0:12]),
                       deps=[b1x, b2x])
            vex = P.op("vector", lambda v, stp=stp, mvv=mvv: v.tensor_scalar(
                out=stp[:, 6:7], in0=mvv[:, 1:2], scalar1=EPS, scalar2=None, op0=Alu.add),
                deps=[agx])
            rsx0 = P.op("scalar", lambda s, stp=stp: s.activation(out=stp[:, 7:8], in_=stp[:, 6:7],
                                                         func=Act.Sqrt), deps=[vex])
            rsx = P.op("vector", lambda v, stp=stp: v.reciprocal(out=stp[:, 7:8], in_=stp[:, 7:8]),
                       deps=[rsx0])
            nrmx = P.op("vector", lambda v, stp=stp, mvv=mvv: v.tensor_scalar(
                out=o_f[:], in0=o_f[:], scalar1=mvv[:, 0:1], scalar2=stp[:, 7:8],
                op0=Alu.subtract, op1=Alu.mult), deps=[rsx])
            resx = P.op("vector", lambda v, i=i: v.tensor_tensor(
                out=o_f[:], in0=o_f[:], in1=x1t[i][:], op=Alu.add), deps=[nrmx])
            x1prev[i] = resx
            pv["oev"] = oev[-1]
            t_out.append(P.op("sync", lambda s, tt2=tt2: s.dma_start(
                out=out_d[tt2 * 128:(tt2 + 1) * 128, :], in_=o_f[:]), deps=[resx], dma=True))

        for rnd in range(ntok + 2):
            if rnd >= 2:
                phaseC(rnd - 2)
            if 1 <= rnd <= ntok:
                phaseB(rnd - 1)
            if rnd < ntok:
                phaseA(rnd)

        fin_s = P.cnt["sync"]
        P.raw("sync", lambda engine, v=fin_s: engine.wait_ge(P.sems["sync"], v))

    P.flush()
    es.close()
    return nc


def prep_inputs(inputs, tc):
    x = np.asarray(inputs["L0_attn_x"], np.float32).reshape(-1, D)
    tp = ((tc + 3 + 127) // 128) * 128

    conv_w = np.asarray(inputs["L0_attn_conv_w"], np.float32).reshape(DI, 4)
    gate_w = np.asarray(inputs["L0_attn_gate_w"], np.float32).reshape(DI, 4)
    convw_all = np.concatenate([conv_w, gate_w], axis=0)
    convb_all = np.concatenate([np.asarray(inputs["L0_attn_conv_b"], np.float32),
                                np.asarray(inputs["L0_attn_gate_b"], np.float32)])

    rep = lambda v: np.ascontiguousarray(
        np.broadcast_to(np.asarray(v, np.float32)[None, :], (128, len(v))))
    Tpose = lambda a: np.ascontiguousarray(np.asarray(a, np.float32).T)

    c1 = {
        "ident": np.eye(128, dtype=np.float32),
        "ones_row": np.ones((1, 128), np.float32),
        "w_in_t": Tpose(inputs["L0_attn_in_proj_w"]),
        "ln_w_col": col_layout(np.asarray(inputs["L0_attn_ln_w"], np.float32)[:, None]),
        "ln_b_col": col_layout(np.asarray(inputs["L0_attn_ln_b"], np.float32)[:, None]),
        "in_b_col": col_layout(np.asarray(inputs["L0_attn_in_proj_b"], np.float32)[:, None]),
        "convw_col": col_layout(convw_all),
        "convb_col": col_layout(convb_all[:, None]),
        "aproj_t": Tpose(inputs["L0_attn_A_proj_w"]),
        "dtw_t": Tpose(inputs["L0_attn_dt_proj_w"]),
        "dtb_col": col_layout(np.asarray(inputs["L0_attn_dt_proj_b"], np.float32)[:, None]),
    }
    w1t_full = Tpose(inputs["L0_ffn_W1"])
    w2t_full = Tpose(inputs["L0_ffn_W2"])
    c3 = {
        "ident": c1["ident"],
        "alog_col": col_layout(np.asarray(inputs["L0_attn_A_log"], np.float32)),
        "bmat_col": col_layout(np.asarray(inputs["L0_attn_B_mat"], np.float32)),
        "cmat_col": col_layout(np.asarray(inputs["L0_attn_C_mat"], np.float32)),
        "dvec_col": col_layout(np.asarray(inputs["L0_attn_D_vec"], np.float32)[:, None]),
        "outw_t": Tpose(inputs["L0_attn_out_proj_w"]),
        "outb_rep": rep(np.asarray(inputs["L0_attn_out_proj_b"], np.float32)),
        "b1_rep": rep(np.asarray(inputs["L0_ffn_b1"], np.float32)),
        "b2_rep": rep(np.asarray(inputs["L0_ffn_b2"], np.float32)),
    }

    maps1, maps3 = [], []
    for k in range(NCORES):
        lo = k * tc
        xtok = np.zeros((tp, D), np.float32)
        if k == 0:
            xtok[3:3 + tc] = x[lo:lo + tc]
        else:
            xtok[0:3 + tc] = x[lo - 3:lo + tc]
        halo = np.ones((128, 4), np.float32)
        if k == 0:
            halo[:, 0:3] = 0.0
        ce = np.full((NCORES,), 1e9, np.float32)
        for j in range(k):
            ce[j] = float(tc * (k - 1 - j))
        m1 = dict(c1)
        m1["x_tok"] = xtok
        m1["halo_mask"] = halo
        m1["w1_sh"] = np.ascontiguousarray(w1t_full[k * 128:(k + 1) * 128])
        m1["w2_sh"] = np.ascontiguousarray(w2t_full[k * (HF // NCORES):(k + 1) * (HF // NCORES)])
        maps1.append(m1)
        m3 = dict(c3)
        m3["x_res"] = np.ascontiguousarray(x[lo:lo + tc])
        m3["chain_e"] = np.ascontiguousarray(np.broadcast_to(ce[None, :], (128, NCORES)))
        maps3.append(m3)
    return maps1, maps3


def kernel(**inputs):
    x = np.asarray(inputs["L0_attn_x"])
    T = x.reshape(-1, D).shape[0]
    tc = T // NCORES
    maps1, maps3 = prep_inputs(inputs, tc)
    cores = list(range(NCORES))

    nc1 = build_nc(tc, 1)
    r1 = run_bass_kernel_spmd(nc1, maps1, core_ids=cores).results

    dt_all = np.ascontiguousarray(np.stack([np.asarray(r["dtp_io"]) for r in r1]))
    ident = maps1[0]["ident"]
    alog = col_layout(np.asarray(inputs["L0_attn_A_log"], np.float32))
    wpart_all = np.ascontiguousarray(np.stack([np.asarray(r["wpart_io"]) for r in r1]))
    revramp = np.ascontiguousarray(np.broadcast_to(
        np.arange(tc - 1, -1, -1, dtype=np.float32)[None, :], (128, tc)))
    maps2 = [{"ident": ident, "alog_col": alog, "dt_all": dt_all,
              "xs_io": np.asarray(r1[i]["xs_io"]),
              "wpart_all": wpart_all, "ones_row": maps1[i]["ones_row"],
              "w1_sh": maps1[i]["w1_sh"], "w2_sh": maps1[i]["w2_sh"],
              "revramp": revramp}
             for i in range(NCORES)]
    nc2 = build_nc(tc, 2)
    r2 = run_bass_kernel_spmd(nc2, maps2, core_ids=cores).results

    f_all = np.ascontiguousarray(np.stack([np.asarray(r["f_io"]) for r in r2]))
    w1q_full = np.ascontiguousarray(np.concatenate([np.asarray(r["w1q_io"]) for r in r2], axis=0))
    w2q_full = np.ascontiguousarray(np.concatenate([np.asarray(r["w2q_io"]) for r in r2], axis=0))
    wsc0 = np.asarray(r2[0]["wsc_io"])
    for i in range(NCORES):
        maps3[i]["dt_all"] = dt_all
        maps3[i]["f_all"] = f_all
        maps3[i]["xs_io"] = np.asarray(r1[i]["xs_io"])
        maps3[i]["z_io"] = np.asarray(r1[i]["z_io"])
        maps3[i]["w1q_io"] = w1q_full
        maps3[i]["w2q_io"] = w2q_full
        maps3[i]["wsc_io"] = wsc0
    nc3 = build_nc(tc, 3)
    r3 = run_bass_kernel_spmd(nc3, maps3, core_ids=cores).results

    out = np.concatenate([np.asarray(r["out"]) for r in r3], axis=0)
    return out.reshape(x.shape).astype(np.float32)
